# version 02
# baseline (speedup 1.0000x reference)
import math

import numpy as np
import ml_dtypes

import concourse.bass as bass
import concourse.mybir as mybir
from concourse.bass_utils import run_bass_kernel_spmd

F32 = mybir.dt.float32
BF16 = mybir.dt.bfloat16
AF = mybir.ActivationFunctionType
ALU = mybir.AluOpType
AX = mybir.AxisListType
bfloat16 = ml_dtypes.bfloat16


def _hadamard(n):
    H = np.array([[1.0]], dtype=np.float64)
    while H.shape[0] < n:
        H = np.block([[H, H], [H, -H]])
    return H


class _Sem:

    def __init__(self, nc, name):
        self.h = nc.alloc_semaphore(name)
        self.n = 0

    def tick(self, instr, k=1):
        instr.then_inc(self.h, k)
        self.n += k
        return self.n


def build_nc(T, D, NH, F, TS, n_cores, debug=False, strict=False, serial_ffn=False,
             debug_nodump=False):
    DC = D // 128
    NTT = T // 128
    NSL = T // TS
    TPS = TS // 128
    C = F // 128
    G = 128 // C
    FW = 512
    NW = F // FW
    BPW = FW // 128
    HG = min(4, NH)
    NG = NH // HG
    NDG = DC // HG
    KW = HG * 128
    S_total = T * n_cores
    assert TS in (128, 256, 512) and NH * 128 == D and C * G == 128

    nc = bass.Bass("TRN2", target_bir_lowering=False,
                   detect_race_conditions=bool(strict),
                   dynamic_dma_scratch_size=16384)

    xin = nc.dram_tensor("xin", [T, D], F32, kind="ExternalInput").ap()
    wq = nc.dram_tensor("wq", [D, D], F32, kind="ExternalInput").ap()
    wk = nc.dram_tensor("wk", [D, D], F32, kind="ExternalInput").ap()
    wv = nc.dram_tensor("wv", [D, D], F32, kind="ExternalInput").ap()
    wo = nc.dram_tensor("wo", [D, D], F32, kind="ExternalInput").ap()
    bq = nc.dram_tensor("bq", [1, D], F32, kind="ExternalInput").ap()
    bk = nc.dram_tensor("bk", [1, D], F32, kind="ExternalInput").ap()
    bv = nc.dram_tensor("bv", [1, D], F32, kind="ExternalInput").ap()
    bo = nc.dram_tensor("bo", [1, D], F32, kind="ExternalInput").ap()
    w1 = nc.dram_tensor("w1", [F, D], F32, kind="ExternalInput").ap()
    w2 = nc.dram_tensor("w2", [D, F], F32, kind="ExternalInput").ap()
    w3 = nc.dram_tensor("w3", [F, D], F32, kind="ExternalInput").ap()
    identb_i = nc.dram_tensor("identb", [128, 128], BF16, kind="ExternalInput").ap()
    h128_i = nc.dram_tensor("h128", [128, 128], BF16, kind="ExternalInput").ap()
    ighc_i = nc.dram_tensor("ighc", [128, 128], BF16, kind="ExternalInput").ap()
    onesb_i = nc.dram_tensor("onesb", [128, 512], BF16, kind="ExternalInput").ap()
    cst_i = nc.dram_tensor("cst", [128, 4], F32, kind="ExternalInput").ap()
    out = nc.dram_tensor("out", [T, D], F32, kind="ExternalOutput").ap()
    if debug or debug_nodump:
        DCT = D // 128
        dbg_xn = nc.dram_tensor("dbg_xn", [128, DCT * T], F32, kind="ExternalOutput").ap()
        dbg_kv = nc.dram_tensor("dbg_kv", [128, NH * 129], F32, kind="ExternalOutput").ap()
        dbg_attn = nc.dram_tensor("dbg_attn", [128, NH * TS], F32, kind="ExternalOutput").ap()
        dbg_x1 = nc.dram_tensor("dbg_x1", [128, D], F32, kind="ExternalOutput").ap()
        dbg_xn2 = nc.dram_tensor("dbg_xn2", [128, (D // 128) * TS], F32, kind="ExternalOutput").ap()
        dbg_h2 = nc.dram_tensor("dbg_h2", [128, (512 // 128) * TS], F32, kind="ExternalOutput").ap()
        dbg_dn = nc.dram_tensor("dbg_dn", [128, (D // 128) * TS], F32, kind="ExternalOutput").ap()
        dbg_w2f = nc.dram_tensor("dbg_w2f", [256, D], F32, kind="ExternalOutput").ap()
        dbg_dn0 = nc.dram_tensor("dbg_dn0", [128, D], F32, kind="ExternalOutput").ap()
        dbg_w1w = nc.dram_tensor("dbg_w1w", [128, (D // 128) * 512], F32, kind="ExternalOutput").ap()
        dbg_hb = nc.dram_tensor("dbg_hb", [128, TS], F32, kind="ExternalOutput").ap()
        dbg_sg = nc.dram_tensor("dbg_sg", [128, TS], F32, kind="ExternalOutput").ap()
        dbg_u = nc.dram_tensor("dbg_u", [128, TS], F32, kind="ExternalOutput").ap()
        dbg_ub = nc.alloc_sbuf_tensor("dbg_ub", [128, TS], F32)
        snap_kv = nc.alloc_sbuf_tensor("snap_kv", [128, NH * 129], F32)
        snap_at = nc.alloc_sbuf_tensor("snap_at", [128, NH * TS], BF16)
        snap_x1 = nc.alloc_sbuf_tensor("snap_x1", [128, D], F32)
        snap_h2 = nc.alloc_sbuf_tensor("snap_h2", [128, 4 * TS], BF16)
        snap_kv2 = nc.alloc_sbuf_tensor("snap_kv2", [128, NH * 129], F32)
        snap_xn = nc.alloc_sbuf_tensor("snap_xn", [128, (D // 128) * T], BF16)
        dbg_xnr = nc.dram_tensor("dbg_xnr", [128, (D // 128) * T], F32, kind="ExternalOutput").ap()
        dbg_kv2 = nc.dram_tensor("dbg_kv2", [128, NH * 129], F32, kind="ExternalOutput").ap()
        dbg_xb = nc.alloc_sbuf_tensor("dbg_xb", [128, 512], F32)
        dbg_w3w = nc.dram_tensor("dbg_w3w", [128, (D // 128) * 512], F32, kind="ExternalOutput").ap()
        dbg_xbo = nc.dram_tensor("dbg_xbo", [128, 512], F32, kind="ExternalOutput").ap()
        dbg_xnb2 = nc.dram_tensor("dbg_xnb2", [128, (D // 128) * TS], F32, kind="ExternalOutput").ap()
        dbg_sem = nc.alloc_semaphore("dbg_sem")
        dbg_cnt = [0]

        def dump(eng_fn, dst, src_ap, wait=None):
            if debug_nodump:
                return

            def _(e):
                if wait is not None:
                    e.wait_ge(*wait)
                e.dma_start(out=dst, in_=src_ap).then_inc(dbg_sem, 16)
                dbg_cnt[0] += 16
                e.wait_ge(dbg_sem, dbg_cnt[0])
            eng_fn(_)

    wqbf = nc.dram_tensor("wqbf", [D, D], BF16, kind="Internal").ap()
    wkbf = nc.dram_tensor("wkbf", [D, D], BF16, kind="Internal").ap()
    wvbf = nc.dram_tensor("wvbf", [D, D], BF16, kind="Internal").ap()
    wobf = nc.dram_tensor("wobf", [D, D], BF16, kind="Internal").ap()
    w1bf = nc.dram_tensor("w1bf", [F, D], BF16, kind="Internal").ap()
    w3bf = nc.dram_tensor("w3bf", [F, D], BF16, kind="Internal").ap()
    w2bf = nc.dram_tensor("w2bf", [D, F], BF16, kind="Internal").ap()
    w2f = nc.dram_tensor("w2f", [F, D], BF16, kind="Internal").ap()
    ccin = nc.dram_tensor("ccin", [128, NH * 129], F32, kind="Internal").ap()
    ccout = nc.dram_tensor("ccout", [128, NH * 129], F32, kind="Internal").ap()

    A = nc.alloc_sbuf_tensor
    xsl = [A(f"xsl{i}", [128, D], F32) for i in range(max(TPS, 2))]
    KGW = DC * max(TS, FW)
    kgrp = A("kgrp", [128, 4 * KGW], BF16)
    projw = [kgrp[:, 0:KGW], kgrp[:, KGW:2 * KGW]]
    xnbuf = kgrp[:, 2 * KGW:3 * KGW]
    attnbuf = kgrp[:, 3 * KGW:4 * KGW]
    wkg = [projw[0], projw[1]]
    wvg = [xnbuf, attnbuf]
    w2tb = [kgrp[:, 0:D], kgrp[:, D:2 * D]]
    w2ob = [kgrp[:, 2 * D:3 * D], kgrp[:, 3 * D:4 * D]]
    XNW = max(DC * T, 4 * DC * FW)
    xnfull_t = A("xnfull", [128, XNW], BF16)
    xnfull = xnfull_t[:, 0:DC * T]
    w1w = [xnfull_t[:, 0:DC * FW], xnfull_t[:, DC * FW:2 * DC * FW]]
    w3w = [xnfull_t[:, 2 * DC * FW:3 * DC * FW], xnfull_t[:, 3 * DC * FW:4 * DC * FW]]
    jy = A("jy", [128, max(NH * 129, D)], F32)
    junk = jy[:, 0:D]
    kvslab = jy[:, 0:NH * 129]
    kb = [A(f"kb{i}", [128, max(TS, KW)], BF16) for i in range(2)]
    vb = [A(f"vb{i}", [128, HG * 129], BF16) for i in range(2)]
    qsb = vb[0][:, 0:TS]
    xntok = [A(f"xntok{i}", [128, D], BF16) for i in range(2)]
    sgb = [xntok[0][:, 0:TS], xntok[0][:, TS:2 * TS]]
    hbb = [xntok[1][:, 0:TS], xntok[1][:, TS:2 * TS]]
    h2b = [A(f"h2b{i}", [128, BPW * TS], BF16) for i in range(2)]
    emb = A("emb", [128, max(TS, KW)], BF16)
    eeb = A("eeb", [128, max(TS, KW)], BF16)
    kvb_t = A("kvb", [128, NH * 128], BF16)
    kvb = kvb_t[:, :]
    ksumb_t = A("ksumb", [128, NH], BF16)
    ksumb = ksumb_t[:, :]
    lncols = A("lncols", [128, 8], F32)
    s_col, ss_col = lncols[:, 0:1], lncols[:, 1:2]
    m_col, m2_col = lncols[:, 2:3], lncols[:, 3:4]
    var_col, lnv_col, rstd_col = lncols[:, 4:5], lncols[:, 5:6], lncols[:, 6:7]
    biasrow = A("biasrow", [1, 2 * D], BF16)
    bkb, bvb = biasrow[0:1, 0:D], biasrow[0:1, D:2 * D]
    bqb, bob = biasrow[0:1, 0:D], biasrow[0:1, D:2 * D]
    rrowb = A("rrowb", [1, TS], BF16)
    qksrow = A("qksrow", [1, TS], F32)
    identb = A("identb_s", [128, 128], BF16)
    h128b = A("h128_s", [128, 128], BF16)
    ighcb = A("ighc_s", [128, 128], BF16)
    onesb = A("onesb_s", [128, 512], BF16)
    cst = A("cst_s", [128, 4], F32)
    eps_ap = cst[:, 0:1]
    nhalf_ap = cst[:, 1:2]
    w2s_ap = cst[:, 2:3]

    ps = [nc.alloc_psum_tensor(f"ps{i}", [128, 512], F32) for i in range(6)]
    pb = [nc.alloc_psum_tensor(f"pb{i}", [128, 512], BF16) for i in range(2)]

    s_const = _Sem(nc, "s_const")
    s_x = [_Sem(nc, f"s_x{i}") for i in range(max(TPS, 2))]
    s_wkg = [_Sem(nc, f"s_wkg{i}") for i in range(2)]
    s_wvg = [_Sem(nc, f"s_wvg{i}") for i in range(2)]
    s_w2t = [_Sem(nc, f"s_w2t{i}") for i in range(2)]
    s_pj = [_Sem(nc, f"s_pj{i}") for i in range(2)]
    s_w1 = [_Sem(nc, f"s_w1{i}") for i in range(2)]
    s_w3 = [_Sem(nc, f"s_w3{i}") for i in range(2)]
    s_w2fw = _Sem(nc, "s_w2fw")
    s_w2f = [_Sem(nc, "s_w2f0"), _Sem(nc, "s_w2f1")]
    s_st = _Sem(nc, "s_st")
    s_wk = _Sem(nc, "s_wk")
    s_wvc = _Sem(nc, "s_wvc")
    s_bias = _Sem(nc, "s_bias")
    s_w2c = _Sem(nc, "s_w2c")
    s_wq = _Sem(nc, "s_wq")
    s_wo = _Sem(nc, "s_wo")
    s_w13 = _Sem(nc, "s_w13")
    s_ccd = _Sem(nc, "s_ccd")
    ccs = _Sem(nc, "ccs")
    pe = _Sem(nc, "pe")
    ve = _Sem(nc, "ve")
    ae = _Sem(nc, "ae")

    block = nc.Block()
    S, Aq, Gq, Tn, V = block.sync, block.scalar, block.gpsimd, block.tensor, block.vector
    rec = {}

    with block:
        def _(s):
            s.dma_start(out=identb[:, :], in_=identb_i).then_inc(s_const.h, 16)
            s.dma_start(out=h128b[:, :], in_=h128_i).then_inc(s_const.h, 16)
            s.dma_start(out=ighcb[:, :], in_=ighc_i).then_inc(s_const.h, 16)
            s.dma_start(out=onesb[:, :], in_=onesb_i).then_inc(s_const.h, 16)
            s.dma_start(out=cst[:, :], in_=cst_i).then_inc(s_const.h, 16)
            s_const.n += 80
        S(_)
        rec["consts"] = s_const.n

        def _(t):
            t.wait_ge(s_const.h, rec["consts"])
        Tn(_)

        def _(a):
            a.wait_ge(s_const.h, rec["consts"])
        Aq(_)

        def _(g):
            g.dma_start(out=wkbf, in_=wk).then_inc(s_wk.h, 16)
            s_wk.n += 16
            rec["wkbf"] = s_wk.n
            g.dma_start(out=wvbf, in_=wv).then_inc(s_wvc.h, 16)
            s_wvc.n += 16
            rec["wvbf"] = s_wvc.n
            g.dma_start(out=biasrow[0:1, 0:D], in_=bk).then_inc(s_bias.h, 16)
            g.dma_start(out=biasrow[0:1, D:2 * D], in_=bv).then_inc(s_bias.h, 16)
            s_bias.n += 32
            rec["biaskv"] = s_bias.n
            nch = max(1, (D * F * 4) >> 24)
            rows = D // nch
            for i in range(nch):
                g.dma_start(out=w2bf[i * rows:(i + 1) * rows, :],
                            in_=w2[i * rows:(i + 1) * rows, :]).then_inc(s_w2c.h, 16)
                s_w2c.n += 16
            rec["w2bf"] = s_w2c.n
            g.dma_start(out=wqbf, in_=wq).then_inc(s_wq.h, 16)
            s_wq.n += 16
            rec["wqbf"] = s_wq.n
            g.dma_start(out=wobf, in_=wo).then_inc(s_wo.h, 16)
            s_wo.n += 16
            rec["wobf"] = s_wo.n
        Gq(_)

        def _(g):
            for wsrc, wdst in ((w1, w1bf), (w3, w3bf)):
                vi = wsrc.rearrange("(j1 b g) d -> b g j1 d", j1=C, b=C, g=G)
                vo = wdst.rearrange("(b g j1) d -> b g j1 d", b=C, g=G, j1=C)
                for b in range(C):
                    g.dma_start(out=vo[b], in_=vi[b]).then_inc(s_w13.h, 16)
                    s_w13.n += 16
            rec["w13bf"] = s_w13.n
        Gq(_)

        def _(v):
            v.wait_ge(s_const.h, rec["consts"])
            for i in range(2):
                for hl in range(HG):
                    ins = v.memset(vb[i][:, hl * 129 + 128:hl * 129 + 129], 1.0)
            ve.tick(ins)
        V(_)
        rec["vones"] = ve.n

        def emit_ln_tile(x_ap, dest_full, dest_T, col_off, buf_i):
            xb = xntok[buf_i]
            dest3 = dest_full.rearrange("p (dc t) -> p dc t", t=dest_T)

            def _(v):
                ins = v.reduce_sum(s_col, x_ap, AX.X)
                ve.tick(ins)
                v.wait_ge(ve.h, ve.n)
                ins = v.tensor_scalar(m_col, s_col, 1.0 / D, None, ALU.mult)
                ve.tick(ins)
                v.wait_ge(ve.h, ve.n)
                ins = v.tensor_mul(m2_col, m_col, m_col)
                ve.tick(ins)
            V(_)
            rec["ln_m"] = ve.n

            def _(a):
                if "ln_var" in rec:
                    a.wait_ge(ve.h, rec["ln_var"])
                ins = a.activation(junk[:, 0:D], x_ap, AF.Square, accum_out=ss_col)
                ae.tick(ins)
            Aq(_)
            rec["ln_ss"] = ae.n

            def _(v):
                v.wait_ge(ae.h, rec["ln_ss"])
                v.wait_ge(ve.h, rec["ln_m"])
                ins = v.scalar_tensor_tensor(var_col, ss_col, 1.0 / D, m2_col,
                                             ALU.mult, ALU.subtract)
                ve.tick(ins)
            V(_)
            rec["ln_var"] = ve.n

            def _(a):
                a.wait_ge(ve.h, rec["ln_var"])
                ins = a.activation(lnv_col, var_col, AF.Ln, bias=eps_ap)
                ae.tick(ins)
                a.wait_ge(ae.h, ae.n)
                ins = a.activation(rstd_col, lnv_col, AF.Exp, scale=nhalf_ap)
                ae.tick(ins)
            Aq(_)
            rec["ln_rstd"] = ae.n

            def _(v):
                v.wait_ge(ae.h, rec["ln_rstd"])
                v.wait_ge(pe.h, rec.get(("xntok_used", buf_i), 0))
                ins = v.tensor_scalar(xb[:, 0:D], x_ap, m_col, rstd_col,
                                      ALU.subtract, ALU.mult)
                ve.tick(ins)
            V(_)
            rec["ln_xn"] = ve.n

            nb = (DC + 3) // 4
            for bch in range(nb):
                dcs = list(range(bch * 4, min(DC, bch * 4 + 4)))

                def _(t, dcs=dcs, bch=bch):
                    t.wait_ge(ve.h, rec["ln_xn"])
                    t.wait_ge(ve.h, rec.get(("pb_used", bch % 2), 0))
                    for i, dc in enumerate(dcs):
                        ins = t.transpose(pb[bch % 2][:, i * 128:(i + 1) * 128],
                                          xb[:, dc * 128:(dc + 1) * 128], identb[:, :])
                    pe.tick(ins)
                Tn(_)
                rec["ln_tr"] = pe.n

                def _(v, dcs=dcs, bch=bch):
                    v.wait_ge(pe.h, rec["ln_tr"])
                    nd = len(dcs)
                    dv = dest3[:, dcs[0]:dcs[0] + nd, col_off:col_off + 128]
                    sv = pb[bch % 2][:, 0:nd * 128].rearrange("p (c q) -> p c q", c=nd)
                    ins = v.tensor_copy(dv, sv)
                    ve.tick(ins)
                V(_)
                rec[("pb_used", bch % 2)] = ve.n
            rec[("xntok_used", buf_i)] = pe.n
            return ve.n

        for tt in range(NTT):
            def _(s, tt=tt):
                s.wait_ge(ve.h, rec.get(("xsl_used_v", tt % 2), 0))
                s.wait_ge(ae.h, rec.get(("xsl_used_a", tt % 2), 0))
                ins = s.dma_start(out=xsl[tt % 2][:, :],
                                  in_=xin[tt * 128:(tt + 1) * 128, :])
                s_x[tt % 2].tick(ins, 16)
            S(_)
            rec["xld"] = s_x[tt % 2].n

            def _(v, tt=tt):
                v.wait_ge(s_x[tt % 2].h, rec["xld"])
            V(_)

            def _(a, tt=tt):
                a.wait_ge(s_x[tt % 2].h, rec["xld"])
            Aq(_)
            last = emit_ln_tile(xsl[tt % 2][:, :], xnfull, T, tt * 128, tt % 2)
            rec[("xsl_used_v", tt % 2)] = last
            rec[("xsl_used_a", tt % 2)] = rec["ln_ss"]
        rec["p1a_v"] = ve.n
        rec["p1a_a"] = ae.n
        if debug:
            def _(v):
                ins = v.tensor_copy(snap_xn[:, :], xnfull)
                ve.tick(ins)
            V(_)
            rec["snap_xn"] = ve.n
            dump(Gq, dbg_xnr, snap_xn[:, :], wait=(ve.h, rec["snap_xn"]))

        kvslab_evac = {}
        for fs in range(NG):
            def _(s, fs=fs):
                s.wait_ge(s_wk.h, rec["wkbf"])
                s.wait_ge(pe.h, rec.get(("wkg_used", fs % 2), 0))
                for dc in range(DC):
                    ins = s.dma_start(out=wkg[fs % 2][:, dc * KW:(dc + 1) * KW],
                                      in_=wkbf[fs * KW:(fs + 1) * KW,
                                               dc * 128:(dc + 1) * 128],
                                      transpose=True)
                    s_wkg[fs % 2].tick(ins, 16)
            S(_)
            rec[("wkg_ld", fs)] = s_wkg[fs % 2].n

            def _(s, fs=fs):
                s.wait_ge(s_wvc.h, rec["wvbf"])
                s.wait_ge(pe.h, rec.get(("wvg_used", fs % 2), 0))
                for dc in range(DC):
                    ins = s.dma_start(out=wvg[fs % 2][:, dc * KW:(dc + 1) * KW],
                                      in_=wvbf[fs * KW:(fs + 1) * KW,
                                               dc * 128:(dc + 1) * 128],
                                      transpose=True)
                    s_wvg[fs % 2].tick(ins, 16)
            S(_)
            rec[("wvg_ld", fs)] = s_wvg[fs % 2].n

            for tt in range(NTT):
                def _(t, fs=fs, tt=tt):
                    if tt == 0:
                        t.wait_ge(s_wkg[fs % 2].h, rec[("wkg_ld", fs)])
                        t.wait_ge(s_wvg[fs % 2].h, rec[("wvg_ld", fs)])
                        t.wait_ge(s_bias.h, rec["biaskv"])
                        t.wait_ge(ve.h, rec["p1a_v"])
                    t.wait_ge(ve.h, rec.get("k_stt", 0))
                    for dc in range(DC):
                        t.matmul(ps[0][:, 0:KW],
                                 xnfull[:, dc * T + tt * 128:dc * T + (tt + 1) * 128],
                                 wkg[fs % 2][:, dc * KW:(dc + 1) * KW],
                                 start=(dc == 0), stop=False)
                    ins = t.matmul(ps[0][:, 0:KW], onesb[0:1, 0:128],
                                   bkb[0:1, fs * KW:(fs + 1) * KW],
                                   start=False, stop=True)
                    pe.tick(ins)
                Tn(_)
                rec["k_mm"] = pe.n

                def _(t, fs=fs, tt=tt):
                    t.wait_ge(ae.h, rec.get("v_evac", 0))
                    for dc in range(DC):
                        t.matmul(ps[1][:, 0:KW],
                                 xnfull[:, dc * T + tt * 128:dc * T + (tt + 1) * 128],
                                 wvg[fs % 2][:, dc * KW:(dc + 1) * KW],
                                 start=(dc == 0), stop=False)
                    ins = t.matmul(ps[1][:, 0:KW], onesb[0:1, 0:128],
                                   bvb[0:1, fs * KW:(fs + 1) * KW],
                                   start=False, stop=True)
                    pe.tick(ins)
                Tn(_)
                rec["v_mm"] = pe.n

                def _(v, fs=fs, tt=tt):
                    v.wait_ge(pe.h, rec["k_mm"])
                    v.wait_ge(ae.h, rec.get("k_ee", 0))
                    ins = v.tensor_scalar(emb[:, 0:KW], ps[0][:, 0:KW],
                                          0.0, None, ALU.min)
                    ve.tick(ins)
                V(_)
                rec["k_em"] = ve.n

                def _(a, fs=fs, tt=tt):
                    a.wait_ge(ve.h, rec["k_em"])
                    a.wait_ge(ve.h, rec.get("k_stt", 0))
                    ins = a.activation(eeb[:, 0:KW], emb[:, 0:KW], AF.Exp)
                    ae.tick(ins)
                Aq(_)
                rec["k_ee"] = ae.n

                def _(v, fs=fs, tt=tt):
                    v.wait_ge(ae.h, rec["k_ee"])
                    v.wait_ge(pe.h, rec.get(("kb_used", tt % 2), 0))
                    ins = v.scalar_tensor_tensor(kb[tt % 2][:, 0:KW],
                                                 ps[0][:, 0:KW], 0.0,
                                                 eeb[:, 0:KW], ALU.max, ALU.add)
                    ve.tick(ins)
                V(_)
                rec["k_stt"] = ve.n

                def _(a, fs=fs, tt=tt):
                    a.wait_ge(pe.h, rec["v_mm"])
                    a.wait_ge(pe.h, rec.get(("vb_used", tt % 2), 0))
                    dv = vb[tt % 2][:, 0:HG * 129].rearrange(
                        "p (h c) -> p h c", h=HG)[:, :, 0:128]
                    sv = ps[1][:, 0:KW].rearrange("p (h c) -> p h c", h=HG)
                    ins = a.activation(dv, sv, AF.Copy)
                    ae.tick(ins)
                Aq(_)
                rec["v_evac"] = ae.n

                def _(t, fs=fs, tt=tt):
                    t.wait_ge(ve.h, rec["k_stt"])
                    t.wait_ge(ae.h, rec["v_evac"])
                    if tt == 0:
                        t.wait_ge(ae.h, rec.get(("pskv_used", fs), 0))
                        t.wait_ge(ve.h, rec["vones"])
                    for hl in range(HG):
                        ins = t.matmul(ps[2 + hl][:, 0:129],
                                       kb[tt % 2][:, hl * 128:(hl + 1) * 128],
                                       vb[tt % 2][:, hl * 129:(hl + 1) * 129],
                                       start=(tt == 0), stop=(tt == NTT - 1))
                    pe.tick(ins)
                Tn(_)
                rec["kv_mm"] = pe.n
                rec[("kb_used", tt % 2)] = pe.n
                rec[("vb_used", tt % 2)] = pe.n
            rec[("wkg_used", fs % 2)] = pe.n
            rec[("wvg_used", fs % 2)] = pe.n

            def _(a, fs=fs):
                a.wait_ge(pe.h, rec["kv_mm"])
                a.wait_ge(ve.h, rec["p1a_v"])
                for hl in range(HG):
                    h = fs * HG + hl
                    ins = a.activation(kvslab[:, h * 129:(h + 1) * 129],
                                       ps[2 + hl][:, 0:129], AF.Copy)
                ae.tick(ins)
            Aq(_)
            kvslab_evac[fs] = ae.n
            rec[("pskv_used", fs + 1)] = ae.n
        rec["p1b_pe"] = pe.n
        rec["p1b_vstt"] = rec["k_stt"]
        rec["p1b_aevac"] = rec["v_evac"]

        if debug:
            def _(v):
                v.wait_ge(ae.h, kvslab_evac[NG - 1])
                ins = v.tensor_copy(snap_kv2[:, :], kvslab)
                ve.tick(ins)
            V(_)
            rec["snap_kv2"] = ve.n
            dump(Gq, dbg_kv2, snap_kv2[:, :], wait=(ve.h, rec["snap_kv2"]))

        def _(g):
            g.wait_ge(ae.h, kvslab_evac[NG - 1])
            g.dma_start(out=ccin, in_=kvslab).then_inc(s_ccd.h, 16)
            s_ccd.n += 16
            g.wait_ge(s_ccd.h, s_ccd.n)
            ins = g.collective_compute(
                "AllReduce", ALU.add, replica_groups=[list(range(n_cores))],
                ins=[ccin], outs=[ccout])
            ccs.tick(ins)
            g.wait_ge(ccs.h, ccs.n)
            g.dma_start(out=kvslab, in_=ccout).then_inc(s_ccd.h, 16)
            s_ccd.n += 16
        Gq(_)
        rec["ccdone"] = s_ccd.n

        for j1 in range(C):
            def _(s, j1=j1):
                if j1 < 2:
                    s.wait_ge(s_w2c.h, rec["w2bf"])
                    s.wait_ge(pe.h, rec["p1b_pe"])
                else:
                    s.wait_ge(pe.h, rec[("w2t_used", j1 % 2)])
                ins = s.dma_start(out=w2tb[j1 % 2][:, 0:D],
                                  in_=w2bf[:, j1 * 128:(j1 + 1) * 128],
                                  transpose=True)
                s_w2t[j1 % 2].tick(ins, 16)
            S(_)
            rec["w2t_ld"] = s_w2t[j1 % 2].n

            nch = (D + 511) // 512
            for c in range(nch):
                cw = min(512, D - c * 512)

                def _(t, j1=j1, c=c, cw=cw):
                    t.wait_ge(s_w2t[j1 % 2].h, rec["w2t_ld"])
                    if j1 == 0 and c < 2:
                        t.wait_ge(ve.h, rec["p1b_vstt"])
                        t.wait_ge(ae.h, rec["p1b_aevac"])
                    else:
                        t.wait_ge(ae.h, rec.get(("w2ps_used", c % 2), 0))
                    ins = t.matmul(ps[c % 2][:, 0:cw], h128b[:, :],
                                   w2tb[j1 % 2][:, c * 512:c * 512 + cw],
                                   start=True, stop=True)
                    pe.tick(ins)
                Tn(_)
                rec["w2mm"] = pe.n

                def _(a, j1=j1, c=c, cw=cw):
                    a.wait_ge(pe.h, rec["w2mm"])
                    if c == 0:
                        a.wait_ge(s_w2f[j1 % 2].h, rec.get(("w2ob_used", j1 % 2), 0))
                    ins = a.activation(w2ob[j1 % 2][:, c * 512:c * 512 + cw],
                                       ps[c % 2][:, 0:cw], AF.Copy, scale=w2s_ap)
                    ae.tick(ins)
                Aq(_)
                rec["w2evac"] = ae.n
                rec[("w2ps_used", c % 2)] = ae.n
            rec[("w2t_used", j1 % 2)] = pe.n

            def _(a, j1=j1):
                a.wait_ge(ae.h, rec["w2evac"])
                vo = w2f.rearrange("(p j1) d -> j1 p d", p=128, j1=C)[j1]
                ins = a.dma_start(out=vo, in_=w2ob[j1 % 2][:, 0:D])
                s_w2f[j1 % 2].tick(ins, 16)
            Aq(_)
            rec[("w2ob_used", j1 % 2)] = s_w2f[j1 % 2].n
        rec["w2f_st"] = (s_w2f[0].n, s_w2f[1].n)
        rec["w2t_pe"] = pe.n
        if debug:
            dump(Gq, dbg_w2f, w2f[0:min(256, F), :], wait=(s_w2f[0].h, rec["w2f_st"][0]))
            dump(Gq, dbg_w2f, w2f[0:min(256, F), :], wait=(s_w2f[1].h, rec["w2f_st"][1]))
        rec["w2_ae"] = ae.n

        def _(v):
            v.wait_ge(s_ccd.h, rec["ccdone"])
            sv = kvslab.rearrange("p (h c) -> p h c", h=NH)
            if debug:
                v.tensor_copy(snap_kv[:, :], kvslab)
            v.tensor_copy(kvb.rearrange("p (h c) -> p h c", h=NH), sv[:, :, 0:128])
            ins = v.tensor_copy(ksumb.rearrange("p (h c) -> p h c", h=NH),
                                sv[:, :, 128:129])
            ve.tick(ins)
        V(_)
        rec["kvb"] = ve.n

        def _(g):
            g.wait_ge(pe.h, rec["p1b_pe"])
            g.dma_start(out=biasrow[0:1, 0:D], in_=bq).then_inc(s_bias.h, 16)
            g.dma_start(out=biasrow[0:1, D:2 * D], in_=bo).then_inc(s_bias.h, 16)
            s_bias.n += 32
        Gq(_)
        rec["biasqo"] = s_bias.n

        NPG = NG + NDG
        rec[("w2fw_used", -1)] = 0
        for sl in range(NSL):
            for tb in range(TPS):
                def _(s, sl=sl, tb=tb):
                    if sl == 0:
                        s.wait_ge(ve.h, rec["p1a_v"])
                        s.wait_ge(ae.h, rec["p1a_a"])
                    else:
                        s.wait_ge(s_st.h, rec["xst_slice"])
                    ins = s.dma_start(
                        out=xsl[tb][:, :],
                        in_=xin[sl * TS + tb * 128:sl * TS + (tb + 1) * 128, :])
                    s_x[tb].tick(ins, 16)
                S(_)
                rec[("xsl_ld", tb)] = s_x[tb].n

            for tb in range(TPS):
                def _(v, sl=sl, tb=tb):
                    v.wait_ge(s_x[tb].h, rec[("xsl_ld", tb)])
                    if sl == 0 and tb == 0:
                        v.wait_ge(pe.h, rec["p1b_pe"])
                V(_)

                def _(a, sl=sl, tb=tb):
                    a.wait_ge(s_x[tb].h, rec[("xsl_ld", tb)])
                    if sl == 0 and tb == 0:
                        a.wait_ge(ve.h, rec["kvb"])
                Aq(_)
                emit_ln_tile(xsl[tb][:, :], xnbuf[:, 0:DC * TS], TS, tb * 128, tb % 2)
            rec[("xn_v", sl)] = ve.n

            for g in range(NG):
                def _(s, sl=sl, g=g):
                    s.wait_ge(s_wq.h, rec["wqbf"])
                    if sl == 0 and g < 2:
                        s.wait_ge(pe.h, rec["w2t_pe"])
                        s.wait_ge(s_w2f[0].h, rec["w2f_st"][0])
                        s.wait_ge(s_w2f[1].h, rec["w2f_st"][1])
                        s.wait_ge(ae.h, rec["w2_ae"])
                    else:
                        s.wait_ge(pe.h, rec[("projw_used", g % 2)])
                    for dc in range(DC):
                        ins = s.dma_start(
                            out=projw[g % 2][:, dc * KW:(dc + 1) * KW],
                            in_=wqbf[g * KW:(g + 1) * KW, dc * 128:(dc + 1) * 128],
                            transpose=True)
                        s_pj[g % 2].tick(ins, 16)
                S(_)
                rec[("projw_ld", g)] = s_pj[g % 2].n

                for hl in range(HG):
                    h = g * HG + hl

                    def _(t, sl=sl, g=g, hl=hl, h=h):
                        if hl == 0:
                            t.wait_ge(s_pj[g % 2].h, rec[("projw_ld", g)])
                            t.wait_ge(ve.h, rec[("xn_v", sl)])
                            if g == 0:
                                t.wait_ge(s_bias.h, rec["biasqo"])
                                t.wait_ge(ve.h, rec.get("p2e_v", 0))
                                t.wait_ge(ve.h, rec.get("dn_all", 0))
                        t.wait_ge(ve.h, rec.get("q_stt", 0))
                        for dc in range(DC):
                            t.matmul(ps[hl % 2][:, 0:TS],
                                     projw[g % 2][:, dc * KW + hl * 128:dc * KW + (hl + 1) * 128],
                                     xnbuf[:, dc * TS:(dc + 1) * TS],
                                     start=(dc == 0), stop=False)
                        ins = t.matmul(ps[hl % 2][:, 0:TS],
                                       bqb[0:1, h * 128:(h + 1) * 128],
                                       onesb[0:1, 0:TS], start=False, stop=True)
                        pe.tick(ins)
                    Tn(_)
                    rec["q_mm"] = pe.n

                    def _(v, hl=hl):
                        v.wait_ge(pe.h, rec["q_mm"])
                        v.wait_ge(ae.h, rec.get("q_ee", 0))
                        ins = v.tensor_scalar(emb[:, 0:TS], ps[hl % 2][:, 0:TS],
                                              0.0, None, ALU.min)
                        ve.tick(ins)
                    V(_)
                    rec["q_em"] = ve.n

                    def _(a, hl=hl):
                        a.wait_ge(ve.h, rec["q_em"])
                        a.wait_ge(ve.h, rec.get("q_stt", 0))
                        ins = a.activation(eeb[:, 0:TS], emb[:, 0:TS], AF.Exp)
                        ae.tick(ins)
                    Aq(_)
                    rec["q_ee"] = ae.n

                    def _(v, hl=hl):
                        v.wait_ge(ae.h, rec["q_ee"])
                        v.wait_ge(pe.h, rec.get(("q1_used", hl % 2), 0))
                        ins = v.scalar_tensor_tensor(kb[hl % 2][:, 0:TS],
                                                     ps[hl % 2][:, 0:TS], 0.0,
                                                     eeb[:, 0:TS], ALU.max, ALU.add)
                        ve.tick(ins)
                    V(_)
                    rec["q_stt"] = ve.n

                    def _(t, hl=hl, h=h):
                        t.wait_ge(ve.h, rec["q_stt"])
                        t.wait_ge(ve.h, rec.get("qks_evac", 0))
                        ins = t.matmul(ps[2][0:1, 0:TS], ksumb[:, h:h + 1],
                                       kb[hl % 2][:, 0:TS], start=True, stop=True)
                        pe.tick(ins)
                    Tn(_)
                    rec["qks_mm"] = pe.n

                    def _(v, h=h):
                        v.wait_ge(pe.h, rec["qks_mm"])
                        ins = v.tensor_scalar(qksrow[0:1, 0:TS], ps[2][0:1, 0:TS],
                                              1e-8, None, ALU.add)
                        ve.tick(ins)
                        v.wait_ge(ve.h, ve.n)
                        v.wait_ge(pe.h, rec.get("rbc_mm", 0))
                        with nc.allow_low_precision("r row is a bf16 matmul operand"):
                            ins = v.reciprocal(rrowb[0:1, 0:TS], qksrow[0:1, 0:TS])
                        ve.tick(ins)
                    V(_)
                    rec["qks_evac"] = ve.n
                    rec["rrow"] = ve.n

                    def _(t, hl=hl):
                        t.wait_ge(ve.h, rec["rrow"])
                        t.wait_ge(ve.h, rec.get("qs_mult", 0))
                        ins = t.matmul(ps[3][:, 0:TS], onesb[0:1, 0:128],
                                       rrowb[0:1, 0:TS], start=True, stop=True)
                        pe.tick(ins)
                    Tn(_)
                    rec["rbc_mm"] = pe.n

                    def _(v, hl=hl):
                        v.wait_ge(pe.h, rec["rbc_mm"])
                        v.wait_ge(pe.h, rec.get("qsb_used", 0))
                        ins = v.tensor_tensor(qsb[:, 0:TS], kb[hl % 2][:, 0:TS],
                                              ps[3][:, 0:TS], ALU.mult)
                        ve.tick(ins)
                    V(_)
                    rec["qs_mult"] = ve.n

                    def _(t, sl=sl, hl=hl, h=h):
                        t.wait_ge(ve.h, rec["qs_mult"])
                        t.wait_ge(ae.h, rec.get(("qkv_used", hl % 2), 0))
                        ins = t.matmul(ps[4 + hl % 2][:, 0:TS],
                                        kvb[:, h * 128:(h + 1) * 128],
                                        qsb[:, 0:TS], start=True, stop=True)
                        pe.tick(ins)
                    Tn(_)
                    rec["qkv_mm"] = pe.n
                    rec[("q1_used", hl % 2)] = pe.n
                    rec["qsb_used"] = pe.n

                    def _(a, sl=sl, h=h):
                        a.wait_ge(pe.h, rec["qkv_mm"])
                        if h == 0:
                            a.wait_ge(pe.h, rec[("w2fw_used", sl - 1)])
                        ins = a.activation(attnbuf[:, h * TS:(h + 1) * TS],
                                           ps[4 + h % 2][:, 0:TS], AF.Copy)
                        ae.tick(ins)
                    Aq(_)
                    rec["attn_evac"] = ae.n
                    rec[("qkv_used", hl % 2)] = ae.n
                rec[("projw_used", g % 2)] = pe.n
            rec["attn_all"] = ae.n
            if debug and sl == 0:
                def _(v):
                    v.wait_ge(ae.h, rec["attn_all"])
                    ins = v.tensor_copy(snap_at[:, :], attnbuf[:, 0:NH * TS])
                    ve.tick(ins)
                V(_)
                rec["snap_at"] = ve.n
                dump(Gq, dbg_attn, snap_at[:, :], wait=(ve.h, rec["snap_at"]))

            for dg in range(NDG):
                gidx = NG + dg

                def _(s, sl=sl, dg=dg, gidx=gidx):
                    s.wait_ge(s_wo.h, rec["wobf"])
                    s.wait_ge(pe.h, rec.get(("projw_used", gidx % 2), 0))
                    for dc in range(DC):
                        ins = s.dma_start(
                            out=projw[gidx % 2][:, dc * KW:(dc + 1) * KW],
                            in_=wobf[dg * KW:(dg + 1) * KW, dc * 128:(dc + 1) * 128],
                            transpose=True)
                        s_pj[gidx % 2].tick(ins, 16)
                S(_)
                rec[("projw_ld", gidx)] = s_pj[gidx % 2].n

                for tb in range(TPS):
                    def _(t, sl=sl, dg=dg, tb=tb, gidx=gidx):
                        if tb == 0:
                            t.wait_ge(s_pj[gidx % 2].h, rec[("projw_ld", gidx)])
                            t.wait_ge(ae.h, rec["attn_all"])
                        t.wait_ge(ve.h, rec.get(("psy_used", tb % 2), 0))
                        for dc in range(DC):
                            t.matmul(ps[tb % 2][:, 0:KW],
                                     attnbuf[:, dc * TS + tb * 128:dc * TS + (tb + 1) * 128],
                                     projw[gidx % 2][:, dc * KW:(dc + 1) * KW],
                                     start=(dc == 0), stop=False)
                        ins = t.matmul(ps[tb % 2][:, 0:KW], onesb[0:1, 0:128],
                                       bob[0:1, dg * KW:(dg + 1) * KW],
                                       start=False, stop=True)
                        pe.tick(ins)
                    Tn(_)
                    rec["y_mm"] = pe.n

                    def _(v, dg=dg, tb=tb):
                        v.wait_ge(pe.h, rec["y_mm"])
                        ins = v.tensor_tensor(
                            xsl[tb][:, dg * KW:(dg + 1) * KW],
                            ps[tb % 2][:, 0:KW],
                            xsl[tb][:, dg * KW:(dg + 1) * KW], ALU.add)
                        ve.tick(ins)
                    V(_)
                    rec[("psy_used", tb % 2)] = ve.n
                rec[("projw_used", gidx % 2)] = pe.n
            rec["resid1_v"] = ve.n
            rec["p2b_pe"] = pe.n
            rec["p2b_ae"] = ae.n

            for tb in range(TPS):
                def _(a, sl=sl, tb=tb):
                    a.wait_ge(ve.h, rec["resid1_v"])
                Aq(_)

                def _(v, sl=sl, tb=tb):
                    v.wait_ge(ve.h, rec["resid1_v"])
                V(_)
                emit_ln_tile(xsl[tb][:, :], xnbuf[:, 0:DC * TS], TS, tb * 128, tb % 2)
            rec[("xn2_v", sl)] = ve.n
            if debug and sl == 0:
                dump(Gq, dbg_xn2, xnbuf[:, 0:DC * TS], wait=(ve.h, rec[("xn2_v", sl)]))

            for w in range(NW):
                def _(s, sl=sl, w=w):
                    s.wait_ge(s_w13.h, rec["w13bf"])
                    if sl == 0 and w < 2:
                        s.wait_ge(pe.h, rec["p1b_pe"])
                    else:
                        s.wait_ge(pe.h, rec.get(("w1w_used", w % 2), 0))
                    for dc in range(DC):
                        ins = s.dma_start(
                            out=w1w[w % 2][:, dc * FW:(dc + 1) * FW],
                            in_=w1bf[w * FW:(w + 1) * FW, dc * 128:(dc + 1) * 128],
                            transpose=True)
                        s_w1[w % 2].tick(ins, 16)
                S(_)
                rec[("w1w_ld", w)] = s_w1[w % 2].n

                def _(s, sl=sl, w=w):
                    s.wait_ge(s_w13.h, rec["w13bf"])
                    if sl == 0 and w < 2:
                        s.wait_ge(pe.h, rec["p1b_pe"])
                    else:
                        s.wait_ge(pe.h, rec.get(("w3w_used", w % 2), 0))
                    for dc in range(DC):
                        ins = s.dma_start(
                            out=w3w[w % 2][:, dc * FW:(dc + 1) * FW],
                            in_=w3bf[w * FW:(w + 1) * FW, dc * 128:(dc + 1) * 128],
                            transpose=True)
                        s_w3[w % 2].tick(ins, 16)
                S(_)
                rec[("w3w_ld", w)] = s_w3[w % 2].n

                def _(g_, sl=sl, w=w):
                    if w == 0:
                        g_.wait_ge(pe.h, rec["p2b_pe"])
                    else:
                        g_.wait_ge(pe.h, rec["dn_mm"])
                    for bl in range(BPW):
                        b = w * BPW + bl
                        ins = g_.dma_start(out=attnbuf[:, bl * D:(bl + 1) * D],
                                           in_=w2f[b * 128:(b + 1) * 128, :])
                        s_w2fw.tick(ins, 16)
                Gq(_)
                rec[("w2fw_ld", w)] = s_w2fw.n

                for bl in range(BPW):
                    def _(t, sl=sl, w=w, bl=bl):
                        if bl == 0:
                            t.wait_ge(s_w1[w % 2].h, rec[("w1w_ld", w)])
                            t.wait_ge(s_w3[w % 2].h, rec[("w3w_ld", w)])
                            t.wait_ge(ve.h, rec[("xn2_v", sl)])
                            if w == 0:
                                t.wait_ge(ae.h, rec["p2b_ae"])
                        t.wait_ge(ae.h, rec.get(("sg", bl % 2), 0))
                        for dc in range(DC):
                            t.matmul(ps[bl % 2][:, 0:TS],
                                     w1w[w % 2][:, dc * FW + bl * 128:dc * FW + (bl + 1) * 128],
                                     xnbuf[:, dc * TS:(dc + 1) * TS],
                                     start=(dc == 0), stop=(dc == DC - 1))
                        t.wait_ge(ve.h, rec.get(("hmul", bl % 2), 0))
                        for dc in range(DC):
                            ins = t.matmul(ps[2 + bl % 2][:, 0:TS],
                                           w3w[w % 2][:, dc * FW + bl * 128:dc * FW + (bl + 1) * 128],
                                           xnbuf[:, dc * TS:(dc + 1) * TS],
                                           start=(dc == 0), stop=(dc == DC - 1))
                        pe.tick(ins)
                    Tn(_)
                    rec["gu_mm"] = pe.n

                    def _(a, w=w, bl=bl):
                        a.wait_ge(pe.h, rec["gu_mm"])
                        a.wait_ge(ve.h, rec.get(("sg_used", bl % 2), 0))
                        ins = a.activation(sgb[bl % 2], ps[bl % 2][:, 0:TS], AF.Silu)
                        ae.tick(ins)
                    Aq(_)
                    rec[("sg", bl % 2)] = ae.n
                    rec["sg_last"] = ae.n

                    def _(v, w=w, bl=bl):
                        if debug and sl == 0 and w == 1 and bl == 2 and "dbg_blk" in rec:
                            v.wait_ge(dbg_sem, rec["dbg_blk"])
                        v.wait_ge(ae.h, rec[("sg", bl % 2)])
                        v.wait_ge(pe.h, rec.get(("hb_used", bl % 2), 0))
                        if debug and sl == 0 and w == 1 and bl == 0:
                            v.tensor_copy(dbg_ub[:, :], ps[2 + bl % 2][:, 0:TS])
                            v.tensor_copy(dbg_xb[:, 0:min(512, DC * TS)],
                                          xnbuf[:, 0:min(512, DC * TS)])
                        ins = v.tensor_tensor(hbb[bl % 2], sgb[bl % 2],
                                              ps[2 + bl % 2][:, 0:TS], ALU.mult)
                        ve.tick(ins)
                    V(_)
                    rec[("hmul", bl % 2)] = ve.n
                    rec[("sg_used", bl % 2)] = ve.n

                    def _(t, sl=sl, w=w, bl=bl):
                        t.wait_ge(ve.h, rec[("hmul", bl % 2)])
                        if w == 0 and bl < 2:
                            t.wait_ge(ae.h, rec["attn_all"])
                        t.wait_ge(ve.h, rec.get(("h2ps_used", bl % 2), 0))
                        ins = t.matmul(ps[4 + bl % 2][:, 0:TS], ighcb[:, :],
                                       hbb[bl % 2], start=True, stop=True)
                        pe.tick(ins)
                    Tn(_)
                    rec["h2_mm"] = pe.n
                    rec[("hb_used", bl % 2)] = pe.n
                    if debug and sl == 0 and w == 1 and bl == 0:
                        dump(Gq, dbg_w1w, w1w[1][:, 0:DC * FW], wait=(ve.h, rec[("hmul", 0)]))
                        dump(Gq, dbg_hb, hbb[0], wait=(ve.h, rec[("hmul", 0)]))
                        dump(Gq, dbg_sg, sgb[0], wait=(ve.h, rec[("hmul", 0)]))
                        dump(Gq, dbg_u, dbg_ub[:, :], wait=(ve.h, rec[("hmul", 0)]))
                        dump(Gq, dbg_xbo, dbg_xb[:, :], wait=(ve.h, rec[("hmul", 0)]))
                        dump(Gq, dbg_w3w, w3w[1][:, 0:DC * FW], wait=(ve.h, rec[("hmul", 0)]))
                        rec["dbg_blk"] = dbg_cnt[0]

                    def _(v, w=w, bl=bl):
                        v.wait_ge(pe.h, rec["h2_mm"])
                        if bl == 0:
                            v.wait_ge(pe.h, rec.get(("h2b_used", w % 2), 0))
                        ins = v.tensor_copy(h2b[w % 2][:, bl * TS:(bl + 1) * TS],
                                            ps[4 + bl % 2][:, 0:TS])
                        ve.tick(ins)
                    V(_)
                    rec["h2cp"] = ve.n
                    rec[("h2ps_used", bl % 2)] = ve.n
                rec[("w1w_used", w % 2)] = pe.n
                rec[("w3w_used", w % 2)] = pe.n
                rec["h2_all"] = ve.n
                if debug and sl == 0 and w == 1:
                    def _(v):
                        ins = v.tensor_copy(snap_h2[:, 0:BPW * TS], h2b[1][:, 0:BPW * TS])
                        ve.tick(ins)
                    V(_)
                    rec["snap_h2"] = ve.n
                    dump(Gq, dbg_h2, snap_h2[:, 0:BPW * TS], wait=(ve.h, rec["snap_h2"]))

                for pi in range(TPS * NDG):
                    tb, dwdw = pi // NDG, pi % NDG

                    def _(t, sl=sl, w=w, tb=tb, dwdw=dwdw, pi=pi):
                        if pi == 0:
                            t.wait_ge(ve.h, rec["h2_all"])
                            t.wait_ge(s_w2fw.h, rec[("w2fw_ld", w)])
                        t.wait_ge(ve.h, rec.get(("dn_used", pi % 2), 0))
                        for bl in range(BPW):
                            ins = t.matmul(
                                ps[4 + pi % 2][:, 0:KW],
                                h2b[w % 2][:, bl * TS + tb * 128:bl * TS + (tb + 1) * 128],
                                attnbuf[:, bl * D + dwdw * KW:bl * D + (dwdw + 1) * KW],
                                start=(bl == 0), stop=(bl == BPW - 1))
                        pe.tick(ins)
                    Tn(_)
                    rec["dn_mm"] = pe.n

                    def _(v, sl=sl, w=w, tb=tb, dwdw=dwdw, pi=pi):
                        v.wait_ge(pe.h, rec["dn_mm"])
                        ins = v.tensor_tensor(
                            xsl[tb][:, dwdw * KW:(dwdw + 1) * KW],
                            ps[4 + pi % 2][:, 0:KW],
                            xsl[tb][:, dwdw * KW:(dwdw + 1) * KW], ALU.add)
                        ve.tick(ins)
                    V(_)
                    rec[("dn_used", pi % 2)] = ve.n
                    if w == NW - 1:
                        rec[("dnadd_tb", tb)] = ve.n
                rec[("h2b_used", w % 2)] = pe.n
                rec[("h2ps_used", 0)] = ve.n
                rec[("h2ps_used", 1)] = ve.n
                if serial_ffn:
                    def _(t, sl=sl, w=w):
                        t.wait_ge(ve.h, ve.n)
                        t.wait_ge(ae.h, ae.n)
                    Tn(_)
                if debug and sl == 0 and w == 0:
                    dump(Gq, dbg_dn0, xsl[0][:, :], wait=(ve.h, ve.n))
            rec[("w2fw_used", sl)] = pe.n
            rec["dn_all"] = ve.n

            for tb in range(TPS):
                def _(a, sl=sl, tb=tb):
                    a.wait_ge(ve.h, rec[("dnadd_tb", tb)])
                    ins = a.dma_start(
                        out=out[sl * TS + tb * 128:sl * TS + (tb + 1) * 128, :],
                        in_=xsl[tb][:, :])
                    s_st.tick(ins, 16)
                Aq(_)
            rec["xst_slice"] = s_st.n
            rec["p2e_v"] = ve.n

        def _(a):
            a.wait_ge(s_st.h, s_st.n)
        Aq(_)

    return nc


_NC_CACHE = {}


def _get_nc(key):
    if key not in _NC_CACHE:
        _NC_CACHE[key] = build_nc(*key)
    return _NC_CACHE[key]


def make_consts(D, F):
    C = F // 128
    G = 128 // C
    return dict(
        identb=np.eye(128).astype(bfloat16),
        h128=_hadamard(128).astype(bfloat16),
        ighc=np.kron(np.eye(G), _hadamard(C)).astype(bfloat16),
        onesb=np.ones((128, 512), dtype=bfloat16),
        cst=np.tile(np.array([[1e-5, -0.5, 1.0 / math.sqrt(F), 0.0]],
                             dtype=np.float32), (128, 1)),
    )


def kernel(L0_attn_x, L0_attn_wq, L0_attn_bq, L0_attn_wk, L0_attn_bk,
           L0_attn_wv, L0_attn_bv, L0_attn_wo, L0_attn_bo,
           L0_ffn_W1, L0_ffn_W2, L0_ffn_W3):
    n_cores = 8
    x = np.ascontiguousarray(np.asarray(L0_attn_x, dtype=np.float32))
    B, Sq, D = x.shape
    T = (B * Sq) // n_cores
    F = np.asarray(L0_ffn_W1).shape[0]
    NH = D // 128
    TS = 512
    xf = x.reshape(B * Sq, D)

    w = {k: np.ascontiguousarray(np.asarray(v, dtype=np.float32)) for k, v in dict(
        wq=L0_attn_wq, wk=L0_attn_wk, wv=L0_attn_wv, wo=L0_attn_wo,
        w1=L0_ffn_W1, w2=L0_ffn_W2, w3=L0_ffn_W3).items()}
    b = {k: np.ascontiguousarray(np.asarray(v, dtype=np.float32).reshape(1, D))
         for k, v in dict(bq=L0_attn_bq, bk=L0_attn_bk, bv=L0_attn_bv,
                          bo=L0_attn_bo).items()}
    consts = make_consts(D, F)

    nc = _get_nc((T, D, NH, F, TS, n_cores))
    in_maps = []
    for c in range(n_cores):
        m = {"xin": np.ascontiguousarray(xf[c * T:(c + 1) * T])}
        m.update(w)
        m.update(b)
        m.update(consts)
        in_maps.append(m)
    res = run_bass_kernel_spmd(nc, in_maps, core_ids=list(range(n_cores)))
    outf = np.concatenate([np.asarray(r["out"]) for r in res.results], axis=0)
    return outf.reshape(B, Sq, D).astype(np.float32)


# version 04 (final)
# speedup vs baseline: 1.0006x; 1.0006x over previous
import math

import numpy as np
import ml_dtypes

import concourse.bass as bass
import concourse.mybir as mybir
from concourse.bass_utils import run_bass_kernel_spmd

F32 = mybir.dt.float32
BF16 = mybir.dt.bfloat16
AF = mybir.ActivationFunctionType
ALU = mybir.AluOpType
AX = mybir.AxisListType
bfloat16 = ml_dtypes.bfloat16


def _hadamard(n):
    H = np.array([[1.0]], dtype=np.float64)
    while H.shape[0] < n:
        H = np.block([[H, H], [H, -H]])
    return H


class _Sem:

    def __init__(self, nc, name):
        self.h = nc.alloc_semaphore(name)
        self.n = 0

    def tick(self, instr, k=1):
        instr.then_inc(self.h, k)
        self.n += k
        return self.n


def build_nc(T, D, NH, F, TS, n_cores, debug=False, strict=False, serial_ffn=False,
             debug_nodump=False):
    DC = D // 128
    NTT = T // 128
    NSL = T // TS
    TPS = TS // 128
    C = F // 128
    G = 128 // C
    FW = 512
    NW = F // FW
    BPW = FW // 128
    HG = min(4, NH)
    NG = NH // HG
    NDG = DC // HG
    KW = HG * 128
    S_total = T * n_cores
    assert TS in (128, 256, 512) and NH * 128 == D and C * G == 128

    nc = bass.Bass("TRN2", target_bir_lowering=False,
                   detect_race_conditions=bool(strict),
                   dynamic_dma_scratch_size=16384)

    xin = nc.dram_tensor("xin", [T, D], F32, kind="ExternalInput").ap()
    wq = nc.dram_tensor("wq", [D, D], F32, kind="ExternalInput").ap()
    wk = nc.dram_tensor("wk", [D, D], F32, kind="ExternalInput").ap()
    wv = nc.dram_tensor("wv", [D, D], F32, kind="ExternalInput").ap()
    wo = nc.dram_tensor("wo", [D, D], F32, kind="ExternalInput").ap()
    bq = nc.dram_tensor("bq", [1, D], F32, kind="ExternalInput").ap()
    bk = nc.dram_tensor("bk", [1, D], F32, kind="ExternalInput").ap()
    bv = nc.dram_tensor("bv", [1, D], F32, kind="ExternalInput").ap()
    bo = nc.dram_tensor("bo", [1, D], F32, kind="ExternalInput").ap()
    w1 = nc.dram_tensor("w1", [F, D], F32, kind="ExternalInput").ap()
    w2 = nc.dram_tensor("w2", [D, F], F32, kind="ExternalInput").ap()
    w3 = nc.dram_tensor("w3", [F, D], F32, kind="ExternalInput").ap()
    identb_i = nc.dram_tensor("identb", [128, 128], BF16, kind="ExternalInput").ap()
    h128_i = nc.dram_tensor("h128", [128, 128], BF16, kind="ExternalInput").ap()
    ighc_i = nc.dram_tensor("ighc", [128, 128], BF16, kind="ExternalInput").ap()
    onesb_i = nc.dram_tensor("onesb", [128, 512], BF16, kind="ExternalInput").ap()
    cst_i = nc.dram_tensor("cst", [128, 4], F32, kind="ExternalInput").ap()
    out = nc.dram_tensor("out", [T, D], F32, kind="ExternalOutput").ap()
    if debug or debug_nodump:
        DCT = D // 128
        dbg_xn = nc.dram_tensor("dbg_xn", [128, DCT * T], F32, kind="ExternalOutput").ap()
        dbg_kv = nc.dram_tensor("dbg_kv", [128, NH * 129], F32, kind="ExternalOutput").ap()
        dbg_attn = nc.dram_tensor("dbg_attn", [128, NH * TS], F32, kind="ExternalOutput").ap()
        dbg_x1 = nc.dram_tensor("dbg_x1", [128, D], F32, kind="ExternalOutput").ap()
        dbg_xn2 = nc.dram_tensor("dbg_xn2", [128, (D // 128) * TS], F32, kind="ExternalOutput").ap()
        dbg_h2 = nc.dram_tensor("dbg_h2", [128, (512 // 128) * TS], F32, kind="ExternalOutput").ap()
        dbg_dn = nc.dram_tensor("dbg_dn", [128, (D // 128) * TS], F32, kind="ExternalOutput").ap()
        dbg_w2f = nc.dram_tensor("dbg_w2f", [256, D], F32, kind="ExternalOutput").ap()
        dbg_dn0 = nc.dram_tensor("dbg_dn0", [128, D], F32, kind="ExternalOutput").ap()
        dbg_w1w = nc.dram_tensor("dbg_w1w", [128, (D // 128) * 512], F32, kind="ExternalOutput").ap()
        dbg_hb = nc.dram_tensor("dbg_hb", [128, TS], F32, kind="ExternalOutput").ap()
        dbg_sg = nc.dram_tensor("dbg_sg", [128, TS], F32, kind="ExternalOutput").ap()
        dbg_u = nc.dram_tensor("dbg_u", [128, TS], F32, kind="ExternalOutput").ap()
        dbg_ub = nc.alloc_sbuf_tensor("dbg_ub", [128, TS], F32)
        snap_kv = nc.alloc_sbuf_tensor("snap_kv", [128, NH * 129], F32)
        snap_at = nc.alloc_sbuf_tensor("snap_at", [128, NH * TS], BF16)
        snap_x1 = nc.alloc_sbuf_tensor("snap_x1", [128, D], F32)
        snap_h2 = nc.alloc_sbuf_tensor("snap_h2", [128, 4 * TS], BF16)
        snap_kv2 = nc.alloc_sbuf_tensor("snap_kv2", [128, NH * 129], F32)
        snap_xn = nc.alloc_sbuf_tensor("snap_xn", [128, (D // 128) * T], BF16)
        dbg_xnr = nc.dram_tensor("dbg_xnr", [128, (D // 128) * T], F32, kind="ExternalOutput").ap()
        dbg_kv2 = nc.dram_tensor("dbg_kv2", [128, NH * 129], F32, kind="ExternalOutput").ap()
        dbg_xb = nc.alloc_sbuf_tensor("dbg_xb", [128, 512], F32)
        dbg_w3w = nc.dram_tensor("dbg_w3w", [128, (D // 128) * 512], F32, kind="ExternalOutput").ap()
        dbg_xbo = nc.dram_tensor("dbg_xbo", [128, 512], F32, kind="ExternalOutput").ap()
        dbg_xnb2 = nc.dram_tensor("dbg_xnb2", [128, (D // 128) * TS], F32, kind="ExternalOutput").ap()
        dbg_sem = nc.alloc_semaphore("dbg_sem")
        dbg_cnt = [0]

        def dump(eng_fn, dst, src_ap, wait=None):
            if debug_nodump:
                return

            def _(e):
                if wait is not None:
                    e.wait_ge(*wait)
                e.dma_start(out=dst, in_=src_ap).then_inc(dbg_sem, 16)
                dbg_cnt[0] += 16
                e.wait_ge(dbg_sem, dbg_cnt[0])
            eng_fn(_)

    wqbf = nc.dram_tensor("wqbf", [D, D], BF16, kind="Internal").ap()
    wkbf = nc.dram_tensor("wkbf", [D, D], BF16, kind="Internal").ap()
    wvbf = nc.dram_tensor("wvbf", [D, D], BF16, kind="Internal").ap()
    wobf = nc.dram_tensor("wobf", [D, D], BF16, kind="Internal").ap()
    w1bf = nc.dram_tensor("w1bf", [F, D], BF16, kind="Internal").ap()
    w3bf = nc.dram_tensor("w3bf", [F, D], BF16, kind="Internal").ap()
    w2bf = nc.dram_tensor("w2bf", [D, F], BF16, kind="Internal").ap()
    w2f = nc.dram_tensor("w2f", [F, D], BF16, kind="Internal").ap()
    w1T7 = nc.dram_tensor("w1T7", [128, (F // 512) * DC * 512], BF16, kind="Internal").ap()
    w3T7 = nc.dram_tensor("w3T7", [128, (F // 512) * DC * 512], BF16, kind="Internal").ap()
    wqT7 = nc.dram_tensor("wqT7", [128, (NH // 4) * DC * 512], BF16, kind="Internal").ap()
    woT7 = nc.dram_tensor("woT7", [128, (D // 512) * DC * 512], BF16, kind="Internal").ap()
    ccin = nc.dram_tensor("ccin", [128, NH * 129], F32, kind="Internal").ap()
    ccout = nc.dram_tensor("ccout", [128, NH * 129], F32, kind="Internal").ap()

    A = nc.alloc_sbuf_tensor
    xsl = [A(f"xsl{i}", [128, D], F32) for i in range(max(TPS, 2))]
    KGW = DC * max(TS, FW)
    kgrp = A("kgrp", [128, 4 * KGW], BF16)
    projw = [kgrp[:, 0:KGW], kgrp[:, KGW:2 * KGW]]
    xnbuf = kgrp[:, 2 * KGW:3 * KGW]
    attnbuf = kgrp[:, 3 * KGW:4 * KGW]
    wkg = [projw[0], projw[1]]
    wvg = [xnbuf, attnbuf]
    w2tb = [kgrp[:, 0:D], kgrp[:, D:2 * D]]
    w2ob = [kgrp[:, 2 * D:3 * D], kgrp[:, 3 * D:4 * D]]
    XNW = max(DC * T, 4 * DC * FW)
    xnfull_t = A("xnfull", [128, XNW], BF16)
    xnfull = xnfull_t[:, 0:DC * T]
    w1w = [xnfull_t[:, 0:DC * FW], xnfull_t[:, DC * FW:2 * DC * FW]]
    w3w = [xnfull_t[:, 2 * DC * FW:3 * DC * FW], xnfull_t[:, 3 * DC * FW:4 * DC * FW]]
    jy = A("jy", [128, max(NH * 129, D)], F32)
    junk = jy[:, 0:D]
    kvslab = jy[:, 0:NH * 129]
    kb = [A(f"kb{i}", [128, max(TS, KW)], BF16) for i in range(2)]
    vb = [A(f"vb{i}", [128, HG * 129], BF16) for i in range(2)]
    qsb = vb[0][:, 0:TS]
    xntok = [A(f"xntok{i}", [128, D], BF16) for i in range(2)]
    sgb = [xntok[0][:, 0:TS], xntok[0][:, TS:2 * TS]]
    hbb = [xntok[1][:, 0:TS], xntok[1][:, TS:2 * TS]]
    h2b = [A(f"h2b{i}", [128, BPW * TS], BF16) for i in range(2)]
    emb = A("emb", [128, max(TS, KW)], BF16)
    eeb = A("eeb", [128, max(TS, KW)], BF16)
    kvb_t = A("kvb", [128, NH * 128], BF16)
    kvb = kvb_t[:, :]
    ksumb_t = A("ksumb", [128, NH], BF16)
    ksumb = ksumb_t[:, :]
    lncols = A("lncols", [128, 8], F32)
    s_col, ss_col = lncols[:, 0:1], lncols[:, 1:2]
    m_col, m2_col = lncols[:, 2:3], lncols[:, 3:4]
    var_col, lnv_col, rstd_col = lncols[:, 4:5], lncols[:, 5:6], lncols[:, 6:7]
    biasrow = A("biasrow", [1, 2 * D], BF16)
    bkb, bvb = biasrow[0:1, 0:D], biasrow[0:1, D:2 * D]
    bqb, bob = biasrow[0:1, 0:D], biasrow[0:1, D:2 * D]
    rrowb = A("rrowb", [1, TS], BF16)
    qksrow = A("qksrow", [1, TS], F32)
    identb = A("identb_s", [128, 128], BF16)
    h128b = A("h128_s", [128, 128], BF16)
    ighcb = A("ighc_s", [128, 128], BF16)
    onesb = A("onesb_s", [128, 512], BF16)
    cst = A("cst_s", [128, 4], F32)
    eps_ap = cst[:, 0:1]
    nhalf_ap = cst[:, 1:2]
    w2s_ap = cst[:, 2:3]

    ps = [nc.alloc_psum_tensor(f"ps{i}", [128, 512], F32) for i in range(6)]
    pb = [nc.alloc_psum_tensor(f"pb{i}", [128, 512], BF16) for i in range(2)]

    s_const = _Sem(nc, "s_const")
    s_x = [_Sem(nc, f"s_x{i}") for i in range(max(TPS, 2))]
    s_wkg = [_Sem(nc, f"s_wkg{i}") for i in range(2)]
    s_wvg = [_Sem(nc, f"s_wvg{i}") for i in range(2)]
    s_w2t = [_Sem(nc, f"s_w2t{i}") for i in range(2)]
    s_pj = [_Sem(nc, f"s_pj{i}") for i in range(2)]
    s_w1 = [_Sem(nc, f"s_w1{i}") for i in range(2)]
    s_w3 = [_Sem(nc, f"s_w3{i}") for i in range(2)]
    s_w2fw = _Sem(nc, "s_w2fw")
    s_w2f = [_Sem(nc, "s_w2f0"), _Sem(nc, "s_w2f1")]
    s_st = _Sem(nc, "s_st")
    s_wk = _Sem(nc, "s_wk")
    s_wvc = _Sem(nc, "s_wvc")
    s_bias = _Sem(nc, "s_bias")
    s_w2c = _Sem(nc, "s_w2c")
    s_wq = _Sem(nc, "s_wq")
    s_wo = _Sem(nc, "s_wo")
    s_w13 = _Sem(nc, "s_w13")
    s_ccd = _Sem(nc, "s_ccd")
    s_t71 = [_Sem(nc, f"s_t71_{i}") for i in range(2)]
    s_t73 = [_Sem(nc, f"s_t73_{i}") for i in range(2)]
    s_t7q = [_Sem(nc, f"s_t7q_{i}") for i in range(2)]
    ccs = _Sem(nc, "ccs")
    pe = _Sem(nc, "pe")
    ve = _Sem(nc, "ve")
    ae = _Sem(nc, "ae")

    block = nc.Block()
    S, Aq, Gq, Tn, V = block.sync, block.scalar, block.gpsimd, block.tensor, block.vector
    rec = {}

    with block:
        def _(s):
            s.dma_start(out=identb[:, :], in_=identb_i).then_inc(s_const.h, 16)
            s.dma_start(out=h128b[:, :], in_=h128_i).then_inc(s_const.h, 16)
            s.dma_start(out=ighcb[:, :], in_=ighc_i).then_inc(s_const.h, 16)
            s.dma_start(out=onesb[:, :], in_=onesb_i).then_inc(s_const.h, 16)
            s.dma_start(out=cst[:, :], in_=cst_i).then_inc(s_const.h, 16)
            s_const.n += 80
        S(_)
        rec["consts"] = s_const.n

        def _(t):
            t.wait_ge(s_const.h, rec["consts"])
        Tn(_)

        def _(a):
            a.wait_ge(s_const.h, rec["consts"])
        Aq(_)

        def _(g):
            g.dma_start(out=wkbf, in_=wk).then_inc(s_wk.h, 16)
            s_wk.n += 16
            rec["wkbf"] = s_wk.n
            g.dma_start(out=wvbf, in_=wv).then_inc(s_wvc.h, 16)
            s_wvc.n += 16
            rec["wvbf"] = s_wvc.n
            g.dma_start(out=biasrow[0:1, 0:D], in_=bk).then_inc(s_bias.h, 16)
            g.dma_start(out=biasrow[0:1, D:2 * D], in_=bv).then_inc(s_bias.h, 16)
            s_bias.n += 32
            rec["biaskv"] = s_bias.n
            nch = max(1, (D * F * 4) >> 24)
            rows = D // nch
            for i in range(nch):
                g.dma_start(out=w2bf[i * rows:(i + 1) * rows, :],
                            in_=w2[i * rows:(i + 1) * rows, :]).then_inc(s_w2c.h, 16)
                s_w2c.n += 16
            rec["w2bf"] = s_w2c.n
            g.dma_start(out=wqbf, in_=wq).then_inc(s_wq.h, 16)
            s_wq.n += 16
            rec["wqbf"] = s_wq.n
            g.dma_start(out=wobf, in_=wo).then_inc(s_wo.h, 16)
            s_wo.n += 16
            rec["wobf"] = s_wo.n
        Gq(_)

        def _(g):
            for wsrc, wdst in ((w1, w1bf), (w3, w3bf)):
                vi = wsrc.rearrange("(j1 b g) d -> b g j1 d", j1=C, b=C, g=G)
                vo = wdst.rearrange("(b g j1) d -> b g j1 d", b=C, g=G, j1=C)
                for b in range(C):
                    g.dma_start(out=vo[b], in_=vi[b]).then_inc(s_w13.h, 16)
                    s_w13.n += 16
            rec["w13bf"] = s_w13.n
        Gq(_)

        def _(v):
            v.wait_ge(s_const.h, rec["consts"])
            for i in range(2):
                for hl in range(HG):
                    ins = v.memset(vb[i][:, hl * 129 + 128:hl * 129 + 129], 1.0)
            ve.tick(ins)
        V(_)
        rec["vones"] = ve.n

        def emit_ln_tile(x_ap, dest_full, dest_T, col_off, buf_i):
            xb = xntok[buf_i]
            dest3 = dest_full.rearrange("p (dc t) -> p dc t", t=dest_T)

            def _(v):
                ins = v.reduce_sum(s_col, x_ap, AX.X)
                ve.tick(ins)
                v.wait_ge(ve.h, ve.n)
                ins = v.tensor_scalar(m_col, s_col, 1.0 / D, None, ALU.mult)
                ve.tick(ins)
                v.wait_ge(ve.h, ve.n)
                ins = v.tensor_mul(m2_col, m_col, m_col)
                ve.tick(ins)
            V(_)
            rec["ln_m"] = ve.n

            def _(a):
                if "ln_var" in rec:
                    a.wait_ge(ve.h, rec["ln_var"])
                ins = a.activation(junk[:, 0:D], x_ap, AF.Square, accum_out=ss_col)
                ae.tick(ins)
            Aq(_)
            rec["ln_ss"] = ae.n

            def _(v):
                v.wait_ge(ae.h, rec["ln_ss"])
                v.wait_ge(ve.h, rec["ln_m"])
                ins = v.scalar_tensor_tensor(var_col, ss_col, 1.0 / D, m2_col,
                                             ALU.mult, ALU.subtract)
                ve.tick(ins)
            V(_)
            rec["ln_var"] = ve.n

            def _(a):
                a.wait_ge(ve.h, rec["ln_var"])
                ins = a.activation(lnv_col, var_col, AF.Ln, bias=eps_ap)
                ae.tick(ins)
                a.wait_ge(ae.h, ae.n)
                ins = a.activation(rstd_col, lnv_col, AF.Exp, scale=nhalf_ap)
                ae.tick(ins)
            Aq(_)
            rec["ln_rstd"] = ae.n

            def _(v):
                v.wait_ge(ae.h, rec["ln_rstd"])
                v.wait_ge(pe.h, rec.get(("xntok_used", buf_i), 0))
                ins = v.tensor_scalar(xb[:, 0:D], x_ap, m_col, rstd_col,
                                      ALU.subtract, ALU.mult)
                ve.tick(ins)
            V(_)
            rec["ln_xn"] = ve.n

            nb = (DC + 3) // 4
            for bch in range(nb):
                dcs = list(range(bch * 4, min(DC, bch * 4 + 4)))

                def _(t, dcs=dcs, bch=bch):
                    t.wait_ge(ve.h, rec["ln_xn"])
                    t.wait_ge(ve.h, rec.get(("pb_used", bch % 2), 0))
                    for i, dc in enumerate(dcs):
                        ins = t.transpose(pb[bch % 2][:, i * 128:(i + 1) * 128],
                                          xb[:, dc * 128:(dc + 1) * 128], identb[:, :])
                    pe.tick(ins)
                Tn(_)
                rec["ln_tr"] = pe.n

                def _(v, dcs=dcs, bch=bch):
                    v.wait_ge(pe.h, rec["ln_tr"])
                    nd = len(dcs)
                    dv = dest3[:, dcs[0]:dcs[0] + nd, col_off:col_off + 128]
                    sv = pb[bch % 2][:, 0:nd * 128].rearrange("p (c q) -> p c q", c=nd)
                    ins = v.tensor_copy(dv, sv)
                    ve.tick(ins)
                V(_)
                rec[("pb_used", bch % 2)] = ve.n
            rec[("xntok_used", buf_i)] = pe.n
            return ve.n

        for tt in range(NTT):
            def _(s, tt=tt):
                s.wait_ge(ve.h, rec.get(("xsl_used_v", tt % 2), 0))
                s.wait_ge(ae.h, rec.get(("xsl_used_a", tt % 2), 0))
                ins = s.dma_start(out=xsl[tt % 2][:, :],
                                  in_=xin[tt * 128:(tt + 1) * 128, :])
                s_x[tt % 2].tick(ins, 16)
            S(_)
            rec["xld"] = s_x[tt % 2].n

            def _(v, tt=tt):
                v.wait_ge(s_x[tt % 2].h, rec["xld"])
            V(_)

            def _(a, tt=tt):
                a.wait_ge(s_x[tt % 2].h, rec["xld"])
            Aq(_)
            last = emit_ln_tile(xsl[tt % 2][:, :], xnfull, T, tt * 128, tt % 2)
            rec[("xsl_used_v", tt % 2)] = last
            rec[("xsl_used_a", tt % 2)] = rec["ln_ss"]
        rec["p1a_v"] = ve.n
        rec["p1a_a"] = ae.n
        if debug:
            def _(v):
                ins = v.tensor_copy(snap_xn[:, :], xnfull)
                ve.tick(ins)
            V(_)
            rec["snap_xn"] = ve.n
            dump(Gq, dbg_xnr, snap_xn[:, :], wait=(ve.h, rec["snap_xn"]))

        kvslab_evac = {}
        for fs in range(NG):
            def _(s, fs=fs):
                s.wait_ge(s_wk.h, rec["wkbf"])
                s.wait_ge(pe.h, rec.get(("wkg_used", fs % 2), 0))
                for dc in range(DC):
                    ins = s.dma_start(out=wkg[fs % 2][:, dc * KW:(dc + 1) * KW],
                                      in_=wkbf[fs * KW:(fs + 1) * KW,
                                               dc * 128:(dc + 1) * 128],
                                      transpose=True)
                    s_wkg[fs % 2].tick(ins, 16)
            S(_)
            rec[("wkg_ld", fs)] = s_wkg[fs % 2].n

            def _(s, fs=fs):
                s.wait_ge(s_wvc.h, rec["wvbf"])
                s.wait_ge(pe.h, rec.get(("wvg_used", fs % 2), 0))
                for dc in range(DC):
                    ins = s.dma_start(out=wvg[fs % 2][:, dc * KW:(dc + 1) * KW],
                                      in_=wvbf[fs * KW:(fs + 1) * KW,
                                               dc * 128:(dc + 1) * 128],
                                      transpose=True)
                    s_wvg[fs % 2].tick(ins, 16)
            S(_)
            rec[("wvg_ld", fs)] = s_wvg[fs % 2].n

            for tt in range(NTT):
                def _(t, fs=fs, tt=tt):
                    if tt == 0:
                        t.wait_ge(s_wkg[fs % 2].h, rec[("wkg_ld", fs)])
                        t.wait_ge(s_wvg[fs % 2].h, rec[("wvg_ld", fs)])
                        t.wait_ge(s_bias.h, rec["biaskv"])
                        t.wait_ge(ve.h, rec["p1a_v"])
                    t.wait_ge(ve.h, rec.get("k_stt", 0))
                    for dc in range(DC):
                        t.matmul(ps[0][:, 0:KW],
                                 xnfull[:, dc * T + tt * 128:dc * T + (tt + 1) * 128],
                                 wkg[fs % 2][:, dc * KW:(dc + 1) * KW],
                                 start=(dc == 0), stop=False)
                    ins = t.matmul(ps[0][:, 0:KW], onesb[0:1, 0:128],
                                   bkb[0:1, fs * KW:(fs + 1) * KW],
                                   start=False, stop=True)
                    pe.tick(ins)
                Tn(_)
                rec["k_mm"] = pe.n

                def _(t, fs=fs, tt=tt):
                    t.wait_ge(ae.h, rec.get("v_evac", 0))
                    for dc in range(DC):
                        t.matmul(ps[1][:, 0:KW],
                                 xnfull[:, dc * T + tt * 128:dc * T + (tt + 1) * 128],
                                 wvg[fs % 2][:, dc * KW:(dc + 1) * KW],
                                 start=(dc == 0), stop=False)
                    ins = t.matmul(ps[1][:, 0:KW], onesb[0:1, 0:128],
                                   bvb[0:1, fs * KW:(fs + 1) * KW],
                                   start=False, stop=True)
                    pe.tick(ins)
                Tn(_)
                rec["v_mm"] = pe.n

                def _(v, fs=fs, tt=tt):
                    v.wait_ge(pe.h, rec["k_mm"])
                    v.wait_ge(ae.h, rec.get("k_ee", 0))
                    ins = v.tensor_scalar(emb[:, 0:KW], ps[0][:, 0:KW],
                                          0.0, None, ALU.min)
                    ve.tick(ins)
                V(_)
                rec["k_em"] = ve.n

                def _(a, fs=fs, tt=tt):
                    a.wait_ge(ve.h, rec["k_em"])
                    a.wait_ge(ve.h, rec.get("k_stt", 0))
                    ins = a.activation(eeb[:, 0:KW], emb[:, 0:KW], AF.Exp)
                    ae.tick(ins)
                Aq(_)
                rec["k_ee"] = ae.n

                def _(v, fs=fs, tt=tt):
                    v.wait_ge(ae.h, rec["k_ee"])
                    v.wait_ge(pe.h, rec.get(("kb_used", tt % 2), 0))
                    ins = v.scalar_tensor_tensor(kb[tt % 2][:, 0:KW],
                                                 ps[0][:, 0:KW], 0.0,
                                                 eeb[:, 0:KW], ALU.max, ALU.add)
                    ve.tick(ins)
                V(_)
                rec["k_stt"] = ve.n

                def _(a, fs=fs, tt=tt):
                    a.wait_ge(pe.h, rec["v_mm"])
                    a.wait_ge(pe.h, rec.get(("vb_used", tt % 2), 0))
                    dv = vb[tt % 2][:, 0:HG * 129].rearrange(
                        "p (h c) -> p h c", h=HG)[:, :, 0:128]
                    sv = ps[1][:, 0:KW].rearrange("p (h c) -> p h c", h=HG)
                    ins = a.activation(dv, sv, AF.Copy)
                    ae.tick(ins)
                Aq(_)
                rec["v_evac"] = ae.n

                def _(t, fs=fs, tt=tt):
                    t.wait_ge(ve.h, rec["k_stt"])
                    t.wait_ge(ae.h, rec["v_evac"])
                    if tt == 0:
                        t.wait_ge(ae.h, rec.get(("pskv_used", fs), 0))
                        t.wait_ge(ve.h, rec["vones"])
                    for hl in range(HG):
                        ins = t.matmul(ps[2 + hl][:, 0:129],
                                       kb[tt % 2][:, hl * 128:(hl + 1) * 128],
                                       vb[tt % 2][:, hl * 129:(hl + 1) * 129],
                                       start=(tt == 0), stop=(tt == NTT - 1))
                    pe.tick(ins)
                Tn(_)
                rec["kv_mm"] = pe.n
                rec[("kb_used", tt % 2)] = pe.n
                rec[("vb_used", tt % 2)] = pe.n
            rec[("wkg_used", fs % 2)] = pe.n
            rec[("wvg_used", fs % 2)] = pe.n

            def _(a, fs=fs):
                a.wait_ge(pe.h, rec["kv_mm"])
                a.wait_ge(ve.h, rec["p1a_v"])
                for hl in range(HG):
                    h = fs * HG + hl
                    ins = a.activation(kvslab[:, h * 129:(h + 1) * 129],
                                       ps[2 + hl][:, 0:129], AF.Copy)
                ae.tick(ins)
            Aq(_)
            kvslab_evac[fs] = ae.n
            rec[("pskv_used", fs + 1)] = ae.n
        rec["p1b_pe"] = pe.n
        rec["p1b_vstt"] = rec["k_stt"]
        rec["p1b_aevac"] = rec["v_evac"]

        if debug:
            def _(v):
                v.wait_ge(ae.h, kvslab_evac[NG - 1])
                ins = v.tensor_copy(snap_kv2[:, :], kvslab)
                ve.tick(ins)
            V(_)
            rec["snap_kv2"] = ve.n
            dump(Gq, dbg_kv2, snap_kv2[:, :], wait=(ve.h, rec["snap_kv2"]))

        def _(g):
            g.wait_ge(ae.h, kvslab_evac[NG - 1])
            g.dma_start(out=ccin, in_=kvslab).then_inc(s_ccd.h, 16)
            s_ccd.n += 16
            g.wait_ge(s_ccd.h, s_ccd.n)
            ins = g.collective_compute(
                "AllReduce", ALU.add, replica_groups=[list(range(n_cores))],
                ins=[ccin], outs=[ccout])
            ccs.tick(ins)
            g.wait_ge(ccs.h, ccs.n)
            g.dma_start(out=kvslab, in_=ccout).then_inc(s_ccd.h, 16)
            s_ccd.n += 16
        Gq(_)
        rec["ccdone"] = s_ccd.n

        for j1 in range(C):
            def _(s, j1=j1):
                if j1 < 2:
                    s.wait_ge(s_w2c.h, rec["w2bf"])
                    s.wait_ge(pe.h, rec["p1b_pe"])
                else:
                    s.wait_ge(pe.h, rec[("w2t_used", j1 % 2)])
                ins = s.dma_start(out=w2tb[j1 % 2][:, 0:D],
                                  in_=w2bf[:, j1 * 128:(j1 + 1) * 128],
                                  transpose=True)
                s_w2t[j1 % 2].tick(ins, 16)
            S(_)
            rec["w2t_ld"] = s_w2t[j1 % 2].n

            nch = (D + 511) // 512
            for c in range(nch):
                cw = min(512, D - c * 512)

                def _(t, j1=j1, c=c, cw=cw):
                    t.wait_ge(s_w2t[j1 % 2].h, rec["w2t_ld"])
                    if j1 == 0 and c < 2:
                        t.wait_ge(ve.h, rec["p1b_vstt"])
                        t.wait_ge(ae.h, rec["p1b_aevac"])
                    else:
                        t.wait_ge(ae.h, rec.get(("w2ps_used", c % 2), 0))
                    ins = t.matmul(ps[c % 2][:, 0:cw], h128b[:, :],
                                   w2tb[j1 % 2][:, c * 512:c * 512 + cw],
                                   start=True, stop=True)
                    pe.tick(ins)
                Tn(_)
                rec["w2mm"] = pe.n

                def _(a, j1=j1, c=c, cw=cw):
                    a.wait_ge(pe.h, rec["w2mm"])
                    if c == 0:
                        a.wait_ge(s_w2f[j1 % 2].h, rec.get(("w2ob_used", j1 % 2), 0))
                    ins = a.activation(w2ob[j1 % 2][:, c * 512:c * 512 + cw],
                                       ps[c % 2][:, 0:cw], AF.Copy, scale=w2s_ap)
                    ae.tick(ins)
                Aq(_)
                rec["w2evac"] = ae.n
                rec[("w2ps_used", c % 2)] = ae.n
            rec[("w2t_used", j1 % 2)] = pe.n

            def _(a, j1=j1):
                a.wait_ge(ae.h, rec["w2evac"])
                vo = w2f.rearrange("(p j1) d -> j1 p d", p=128, j1=C)[j1]
                ins = a.dma_start(out=vo, in_=w2ob[j1 % 2][:, 0:D])
                s_w2f[j1 % 2].tick(ins, 16)
            Aq(_)
            rec[("w2ob_used", j1 % 2)] = s_w2f[j1 % 2].n
        rec["w2f_st"] = (s_w2f[0].n, s_w2f[1].n)
        rec["w2t_pe"] = pe.n
        if debug:
            dump(Gq, dbg_w2f, w2f[0:min(256, F), :], wait=(s_w2f[0].h, rec["w2f_st"][0]))
            dump(Gq, dbg_w2f, w2f[0:min(256, F), :], wait=(s_w2f[1].h, rec["w2f_st"][1]))
        rec["w2_ae"] = ae.n

        def _(v):
            v.wait_ge(s_ccd.h, rec["ccdone"])
            sv = kvslab.rearrange("p (h c) -> p h c", h=NH)
            if debug:
                v.tensor_copy(snap_kv[:, :], kvslab)
            v.tensor_copy(kvb.rearrange("p (h c) -> p h c", h=NH), sv[:, :, 0:128])
            ins = v.tensor_copy(ksumb.rearrange("p (h c) -> p h c", h=NH),
                                sv[:, :, 128:129])
            ve.tick(ins)
        V(_)
        rec["kvb"] = ve.n

        def _(g):
            g.wait_ge(pe.h, rec["p1b_pe"])
            g.dma_start(out=biasrow[0:1, 0:D], in_=bq).then_inc(s_bias.h, 16)
            g.dma_start(out=biasrow[0:1, D:2 * D], in_=bo).then_inc(s_bias.h, 16)
            s_bias.n += 32
        Gq(_)
        rec["biasqo"] = s_bias.n

        NPG = NG + NDG
        rec[("w2fw_used", -1)] = 0
        for sl in range(NSL):
            for tb in range(TPS):
                def _(s, sl=sl, tb=tb):
                    if sl == 0:
                        s.wait_ge(ve.h, rec["p1a_v"])
                        s.wait_ge(ae.h, rec["p1a_a"])
                    else:
                        s.wait_ge(s_st.h, rec["xst_slice"])
                    ins = s.dma_start(
                        out=xsl[tb][:, :],
                        in_=xin[sl * TS + tb * 128:sl * TS + (tb + 1) * 128, :])
                    s_x[tb].tick(ins, 16)
                S(_)
                rec[("xsl_ld", tb)] = s_x[tb].n

            for tb in range(TPS):
                def _(v, sl=sl, tb=tb):
                    v.wait_ge(s_x[tb].h, rec[("xsl_ld", tb)])
                    if sl == 0 and tb == 0:
                        v.wait_ge(pe.h, rec["p1b_pe"])
                V(_)

                def _(a, sl=sl, tb=tb):
                    a.wait_ge(s_x[tb].h, rec[("xsl_ld", tb)])
                    if sl == 0 and tb == 0:
                        a.wait_ge(ve.h, rec["kvb"])
                Aq(_)
                emit_ln_tile(xsl[tb][:, :], xnbuf[:, 0:DC * TS], TS, tb * 128, tb % 2)
            rec[("xn_v", sl)] = ve.n

            for g in range(NG):
                def _(s, sl=sl, g=g):
                    s.wait_ge(s_wq.h, rec["wqbf"])
                    if sl == 0 and g < 2:
                        s.wait_ge(pe.h, rec["w2t_pe"])
                        s.wait_ge(s_w2f[0].h, rec["w2f_st"][0])
                        s.wait_ge(s_w2f[1].h, rec["w2f_st"][1])
                        s.wait_ge(ae.h, rec["w2_ae"])
                    else:
                        s.wait_ge(pe.h, rec[("projw_used", g % 2)])
                    if sl > 0:
                        s.wait_ge(s_t7q[g % 2].h, rec[("t7q", g % 2)])
                    if sl == 0:
                        for dc in range(DC):
                            ins = s.dma_start(
                                out=projw[g % 2][:, dc * KW:(dc + 1) * KW],
                                in_=wqbf[g * KW:(g + 1) * KW, dc * 128:(dc + 1) * 128],
                                transpose=True)
                            s_pj[g % 2].tick(ins, 16)
                    else:
                        ins = s.dma_start(out=projw[g % 2][:, :],
                                          in_=wqT7[:, g * DC * KW:(g + 1) * DC * KW])
                        s_pj[g % 2].tick(ins, 16)
                S(_)
                rec[("projw_ld", g)] = s_pj[g % 2].n
                if sl == 0:
                    def _(g_, g=g):
                        g_.wait_ge(s_pj[g % 2].h, rec[("projw_ld", g)])
                        ins = g_.dma_start(out=wqT7[:, g * DC * KW:(g + 1) * DC * KW],
                                           in_=projw[g % 2][:, :])
                        s_t7q[g % 2].tick(ins, 16)
                    Gq(_)
                    rec[("t7q", g % 2)] = s_t7q[g % 2].n

                for hl in range(HG):
                    h = g * HG + hl

                    def _(t, sl=sl, g=g, hl=hl, h=h):
                        if hl == 0:
                            t.wait_ge(s_pj[g % 2].h, rec[("projw_ld", g)])
                            t.wait_ge(ve.h, rec[("xn_v", sl)])
                            if g == 0:
                                t.wait_ge(s_bias.h, rec["biasqo"])
                                t.wait_ge(ve.h, rec.get("p2e_v", 0))
                                t.wait_ge(ve.h, rec.get("dn_all", 0))
                        t.wait_ge(ve.h, rec.get("q_stt", 0))
                        for dc in range(DC):
                            t.matmul(ps[hl % 2][:, 0:TS],
                                     projw[g % 2][:, dc * KW + hl * 128:dc * KW + (hl + 1) * 128],
                                     xnbuf[:, dc * TS:(dc + 1) * TS],
                                     start=(dc == 0), stop=False)
                        ins = t.matmul(ps[hl % 2][:, 0:TS],
                                       bqb[0:1, h * 128:(h + 1) * 128],
                                       onesb[0:1, 0:TS], start=False, stop=True)
                        pe.tick(ins)
                    Tn(_)
                    rec["q_mm"] = pe.n

                    def _(v, hl=hl):
                        v.wait_ge(pe.h, rec["q_mm"])
                        v.wait_ge(ae.h, rec.get("q_ee", 0))
                        ins = v.tensor_scalar(emb[:, 0:TS], ps[hl % 2][:, 0:TS],
                                              0.0, None, ALU.min)
                        ve.tick(ins)
                    V(_)
                    rec["q_em"] = ve.n

                    def _(a, hl=hl):
                        a.wait_ge(ve.h, rec["q_em"])
                        a.wait_ge(ve.h, rec.get("q_stt", 0))
                        ins = a.activation(eeb[:, 0:TS], emb[:, 0:TS], AF.Exp)
                        ae.tick(ins)
                    Aq(_)
                    rec["q_ee"] = ae.n

                    def _(v, hl=hl):
                        v.wait_ge(ae.h, rec["q_ee"])
                        v.wait_ge(pe.h, rec.get(("q1_used", hl % 2), 0))
                        ins = v.scalar_tensor_tensor(kb[hl % 2][:, 0:TS],
                                                     ps[hl % 2][:, 0:TS], 0.0,
                                                     eeb[:, 0:TS], ALU.max, ALU.add)
                        ve.tick(ins)
                    V(_)
                    rec["q_stt"] = ve.n

                    def _(t, hl=hl, h=h):
                        t.wait_ge(ve.h, rec["q_stt"])
                        t.wait_ge(ve.h, rec.get("qks_evac", 0))
                        ins = t.matmul(ps[2][0:1, 0:TS], ksumb[:, h:h + 1],
                                       kb[hl % 2][:, 0:TS], start=True, stop=True)
                        pe.tick(ins)
                    Tn(_)
                    rec["qks_mm"] = pe.n

                    def _(v, h=h):
                        v.wait_ge(pe.h, rec["qks_mm"])
                        ins = v.tensor_scalar(qksrow[0:1, 0:TS], ps[2][0:1, 0:TS],
                                              1e-8, None, ALU.add)
                        ve.tick(ins)
                        v.wait_ge(ve.h, ve.n)
                        v.wait_ge(pe.h, rec.get("rbc_mm", 0))
                        with nc.allow_low_precision("r row is a bf16 matmul operand"):
                            ins = v.reciprocal(rrowb[0:1, 0:TS], qksrow[0:1, 0:TS])
                        ve.tick(ins)
                    V(_)
                    rec["qks_evac"] = ve.n
                    rec["rrow"] = ve.n

                    def _(t, hl=hl):
                        t.wait_ge(ve.h, rec["rrow"])
                        t.wait_ge(ve.h, rec.get("qs_mult", 0))
                        ins = t.matmul(ps[3][:, 0:TS], onesb[0:1, 0:128],
                                       rrowb[0:1, 0:TS], start=True, stop=True)
                        pe.tick(ins)
                    Tn(_)
                    rec["rbc_mm"] = pe.n

                    def _(v, hl=hl):
                        v.wait_ge(pe.h, rec["rbc_mm"])
                        v.wait_ge(pe.h, rec.get("qsb_used", 0))
                        ins = v.tensor_tensor(qsb[:, 0:TS], kb[hl % 2][:, 0:TS],
                                              ps[3][:, 0:TS], ALU.mult)
                        ve.tick(ins)
                    V(_)
                    rec["qs_mult"] = ve.n

                    def _(t, sl=sl, hl=hl, h=h):
                        t.wait_ge(ve.h, rec["qs_mult"])
                        t.wait_ge(ae.h, rec.get(("qkv_used", hl % 2), 0))
                        ins = t.matmul(ps[4 + hl % 2][:, 0:TS],
                                        kvb[:, h * 128:(h + 1) * 128],
                                        qsb[:, 0:TS], start=True, stop=True)
                        pe.tick(ins)
                    Tn(_)
                    rec["qkv_mm"] = pe.n
                    rec[("q1_used", hl % 2)] = pe.n
                    rec["qsb_used"] = pe.n

                    def _(a, sl=sl, h=h):
                        a.wait_ge(pe.h, rec["qkv_mm"])
                        if h == 0:
                            a.wait_ge(pe.h, rec[("w2fw_used", sl - 1)])
                        ins = a.activation(attnbuf[:, h * TS:(h + 1) * TS],
                                           ps[4 + h % 2][:, 0:TS], AF.Copy)
                        ae.tick(ins)
                    Aq(_)
                    rec["attn_evac"] = ae.n
                    rec[("qkv_used", hl % 2)] = ae.n
                rec[("projw_used", g % 2)] = pe.n
            rec["attn_all"] = ae.n
            if debug and sl == 0:
                def _(v):
                    v.wait_ge(ae.h, rec["attn_all"])
                    ins = v.tensor_copy(snap_at[:, :], attnbuf[:, 0:NH * TS])
                    ve.tick(ins)
                V(_)
                rec["snap_at"] = ve.n
                dump(Gq, dbg_attn, snap_at[:, :], wait=(ve.h, rec["snap_at"]))

            for dg in range(NDG):
                gidx = NG + dg

                def _(s, sl=sl, dg=dg, gidx=gidx):
                    s.wait_ge(s_wo.h, rec["wobf"])
                    s.wait_ge(pe.h, rec.get(("projw_used", gidx % 2), 0))
                    s.wait_ge(s_t7q[gidx % 2].h, rec.get(("t7q", gidx % 2), 0))
                    if sl == 0:
                        for dc in range(DC):
                            ins = s.dma_start(
                                out=projw[gidx % 2][:, dc * KW:(dc + 1) * KW],
                                in_=wobf[dg * KW:(dg + 1) * KW, dc * 128:(dc + 1) * 128],
                                transpose=True)
                            s_pj[gidx % 2].tick(ins, 16)
                    else:
                        ins = s.dma_start(out=projw[gidx % 2][:, :],
                                          in_=woT7[:, dg * DC * KW:(dg + 1) * DC * KW])
                        s_pj[gidx % 2].tick(ins, 16)
                S(_)
                rec[("projw_ld", gidx)] = s_pj[gidx % 2].n
                if sl == 0:
                    def _(g_, dg=dg, gidx=gidx):
                        g_.wait_ge(s_pj[gidx % 2].h, rec[("projw_ld", gidx)])
                        ins = g_.dma_start(out=woT7[:, dg * DC * KW:(dg + 1) * DC * KW],
                                           in_=projw[gidx % 2][:, :])
                        s_t7q[gidx % 2].tick(ins, 16)
                    Gq(_)
                    rec[("t7q", gidx % 2)] = s_t7q[gidx % 2].n

                for tb in range(TPS):
                    def _(t, sl=sl, dg=dg, tb=tb, gidx=gidx):
                        if tb == 0:
                            t.wait_ge(s_pj[gidx % 2].h, rec[("projw_ld", gidx)])
                            t.wait_ge(ae.h, rec["attn_all"])
                        t.wait_ge(ve.h, rec.get(("psy_used", tb % 2), 0))
                        for dc in range(DC):
                            t.matmul(ps[tb % 2][:, 0:KW],
                                     attnbuf[:, dc * TS + tb * 128:dc * TS + (tb + 1) * 128],
                                     projw[gidx % 2][:, dc * KW:(dc + 1) * KW],
                                     start=(dc == 0), stop=False)
                        ins = t.matmul(ps[tb % 2][:, 0:KW], onesb[0:1, 0:128],
                                       bob[0:1, dg * KW:(dg + 1) * KW],
                                       start=False, stop=True)
                        pe.tick(ins)
                    Tn(_)
                    rec["y_mm"] = pe.n

                    def _(v, dg=dg, tb=tb):
                        v.wait_ge(pe.h, rec["y_mm"])
                        ins = v.tensor_tensor(
                            xsl[tb][:, dg * KW:(dg + 1) * KW],
                            ps[tb % 2][:, 0:KW],
                            xsl[tb][:, dg * KW:(dg + 1) * KW], ALU.add)
                        ve.tick(ins)
                    V(_)
                    rec[("psy_used", tb % 2)] = ve.n
                rec[("projw_used", gidx % 2)] = pe.n
            rec["resid1_v"] = ve.n
            rec["p2b_pe"] = pe.n
            rec["p2b_ae"] = ae.n

            for tb in range(TPS):
                def _(a, sl=sl, tb=tb):
                    a.wait_ge(ve.h, rec["resid1_v"])
                Aq(_)

                def _(v, sl=sl, tb=tb):
                    v.wait_ge(ve.h, rec["resid1_v"])
                V(_)
                emit_ln_tile(xsl[tb][:, :], xnbuf[:, 0:DC * TS], TS, tb * 128, tb % 2)
            rec[("xn2_v", sl)] = ve.n
            if debug and sl == 0:
                dump(Gq, dbg_xn2, xnbuf[:, 0:DC * TS], wait=(ve.h, rec[("xn2_v", sl)]))

            for w in range(NW):
                def _(s, sl=sl, w=w):
                    s.wait_ge(s_w13.h, rec["w13bf"])
                    if sl == 0 and w < 2:
                        s.wait_ge(pe.h, rec["p1b_pe"])
                    else:
                        s.wait_ge(pe.h, rec.get(("w1w_used", w % 2), 0))
                    s.wait_ge(s_t71[w % 2].h, rec.get(("t71", w % 2), 0))
                    if sl == 0:
                        for dc in range(DC):
                            ins = s.dma_start(
                                out=w1w[w % 2][:, dc * FW:(dc + 1) * FW],
                                in_=w1bf[w * FW:(w + 1) * FW, dc * 128:(dc + 1) * 128],
                                transpose=True)
                            s_w1[w % 2].tick(ins, 16)
                    else:
                        ins = s.dma_start(out=w1w[w % 2][:, :],
                                          in_=w1T7[:, w * DC * FW:(w + 1) * DC * FW])
                        s_w1[w % 2].tick(ins, 16)
                S(_)
                rec[("w1w_ld", w)] = s_w1[w % 2].n
                if sl == 0:
                    def _(g_, w=w):
                        g_.wait_ge(s_w1[w % 2].h, rec[("w1w_ld", w)])
                        ins = g_.dma_start(out=w1T7[:, w * DC * FW:(w + 1) * DC * FW],
                                           in_=w1w[w % 2][:, :])
                        s_t71[w % 2].tick(ins, 16)
                    Gq(_)
                    rec[("t71", w % 2)] = s_t71[w % 2].n

                def _(s, sl=sl, w=w):
                    s.wait_ge(s_w13.h, rec["w13bf"])
                    if sl == 0 and w < 2:
                        s.wait_ge(pe.h, rec["p1b_pe"])
                    else:
                        s.wait_ge(pe.h, rec.get(("w3w_used", w % 2), 0))
                    s.wait_ge(s_t73[w % 2].h, rec.get(("t73", w % 2), 0))
                    if sl == 0:
                        for dc in range(DC):
                            ins = s.dma_start(
                                out=w3w[w % 2][:, dc * FW:(dc + 1) * FW],
                                in_=w3bf[w * FW:(w + 1) * FW, dc * 128:(dc + 1) * 128],
                                transpose=True)
                            s_w3[w % 2].tick(ins, 16)
                    else:
                        ins = s.dma_start(out=w3w[w % 2][:, :],
                                          in_=w3T7[:, w * DC * FW:(w + 1) * DC * FW])
                        s_w3[w % 2].tick(ins, 16)
                S(_)
                rec[("w3w_ld", w)] = s_w3[w % 2].n
                if sl == 0:
                    def _(g_, w=w):
                        g_.wait_ge(s_w3[w % 2].h, rec[("w3w_ld", w)])
                        ins = g_.dma_start(out=w3T7[:, w * DC * FW:(w + 1) * DC * FW],
                                           in_=w3w[w % 2][:, :])
                        s_t73[w % 2].tick(ins, 16)
                    Gq(_)
                    rec[("t73", w % 2)] = s_t73[w % 2].n

                def _(g_, sl=sl, w=w):
                    if w == 0:
                        g_.wait_ge(pe.h, rec["p2b_pe"])
                    else:
                        g_.wait_ge(pe.h, rec["dn_mm"])
                    for bl in range(BPW):
                        b = w * BPW + bl
                        ins = g_.dma_start(out=attnbuf[:, bl * D:(bl + 1) * D],
                                           in_=w2f[b * 128:(b + 1) * 128, :])
                        s_w2fw.tick(ins, 16)
                Gq(_)
                rec[("w2fw_ld", w)] = s_w2fw.n

                for bl in range(BPW):
                    def _(t, sl=sl, w=w, bl=bl):
                        if bl == 0:
                            t.wait_ge(s_w1[w % 2].h, rec[("w1w_ld", w)])
                            t.wait_ge(s_w3[w % 2].h, rec[("w3w_ld", w)])
                            t.wait_ge(ve.h, rec[("xn2_v", sl)])
                            if w == 0:
                                t.wait_ge(ae.h, rec["p2b_ae"])
                        t.wait_ge(ae.h, rec.get(("sg", bl % 2), 0))
                        for dc in range(DC):
                            t.matmul(ps[bl % 2][:, 0:TS],
                                     w1w[w % 2][:, dc * FW + bl * 128:dc * FW + (bl + 1) * 128],
                                     xnbuf[:, dc * TS:(dc + 1) * TS],
                                     start=(dc == 0), stop=(dc == DC - 1))
                        t.wait_ge(ve.h, rec.get(("hmul", bl % 2), 0))
                        for dc in range(DC):
                            ins = t.matmul(ps[2 + bl % 2][:, 0:TS],
                                           w3w[w % 2][:, dc * FW + bl * 128:dc * FW + (bl + 1) * 128],
                                           xnbuf[:, dc * TS:(dc + 1) * TS],
                                           start=(dc == 0), stop=(dc == DC - 1))
                        pe.tick(ins)
                    Tn(_)
                    rec["gu_mm"] = pe.n

                    def _(a, w=w, bl=bl):
                        a.wait_ge(pe.h, rec["gu_mm"])
                        a.wait_ge(ve.h, rec.get(("sg_used", bl % 2), 0))
                        ins = a.activation(sgb[bl % 2], ps[bl % 2][:, 0:TS], AF.Silu)
                        ae.tick(ins)
                    Aq(_)
                    rec[("sg", bl % 2)] = ae.n
                    rec["sg_last"] = ae.n

                    def _(v, w=w, bl=bl):
                        if debug and sl == 0 and w == 1 and bl == 2 and "dbg_blk" in rec:
                            v.wait_ge(dbg_sem, rec["dbg_blk"])
                        v.wait_ge(ae.h, rec[("sg", bl % 2)])
                        v.wait_ge(pe.h, rec.get(("hb_used", bl % 2), 0))
                        if debug and sl == 0 and w == 1 and bl == 0:
                            v.tensor_copy(dbg_ub[:, :], ps[2 + bl % 2][:, 0:TS])
                            v.tensor_copy(dbg_xb[:, 0:min(512, DC * TS)],
                                          xnbuf[:, 0:min(512, DC * TS)])
                        ins = v.tensor_tensor(hbb[bl % 2], sgb[bl % 2],
                                              ps[2 + bl % 2][:, 0:TS], ALU.mult)
                        ve.tick(ins)
                    V(_)
                    rec[("hmul", bl % 2)] = ve.n
                    rec[("sg_used", bl % 2)] = ve.n

                    def _(t, sl=sl, w=w, bl=bl):
                        t.wait_ge(ve.h, rec[("hmul", bl % 2)])
                        if w == 0 and bl < 2:
                            t.wait_ge(ae.h, rec["attn_all"])
                        t.wait_ge(ve.h, rec.get(("h2ps_used", bl % 2), 0))
                        ins = t.matmul(ps[4 + bl % 2][:, 0:TS], ighcb[:, :],
                                       hbb[bl % 2], start=True, stop=True)
                        pe.tick(ins)
                    Tn(_)
                    rec["h2_mm"] = pe.n
                    rec[("hb_used", bl % 2)] = pe.n
                    if debug and sl == 0 and w == 1 and bl == 0:
                        dump(Gq, dbg_w1w, w1w[1][:, 0:DC * FW], wait=(ve.h, rec[("hmul", 0)]))
                        dump(Gq, dbg_hb, hbb[0], wait=(ve.h, rec[("hmul", 0)]))
                        dump(Gq, dbg_sg, sgb[0], wait=(ve.h, rec[("hmul", 0)]))
                        dump(Gq, dbg_u, dbg_ub[:, :], wait=(ve.h, rec[("hmul", 0)]))
                        dump(Gq, dbg_xbo, dbg_xb[:, :], wait=(ve.h, rec[("hmul", 0)]))
                        dump(Gq, dbg_w3w, w3w[1][:, 0:DC * FW], wait=(ve.h, rec[("hmul", 0)]))
                        rec["dbg_blk"] = dbg_cnt[0]

                    def _(v, w=w, bl=bl):
                        v.wait_ge(pe.h, rec["h2_mm"])
                        if bl == 0:
                            v.wait_ge(pe.h, rec.get(("h2b_used", w % 2), 0))
                        ins = v.tensor_copy(h2b[w % 2][:, bl * TS:(bl + 1) * TS],
                                            ps[4 + bl % 2][:, 0:TS])
                        ve.tick(ins)
                    V(_)
                    rec["h2cp"] = ve.n
                    rec[("h2ps_used", bl % 2)] = ve.n
                rec[("w1w_used", w % 2)] = pe.n
                rec[("w3w_used", w % 2)] = pe.n
                rec["h2_all"] = ve.n
                if debug and sl == 0 and w == 1:
                    def _(v):
                        ins = v.tensor_copy(snap_h2[:, 0:BPW * TS], h2b[1][:, 0:BPW * TS])
                        ve.tick(ins)
                    V(_)
                    rec["snap_h2"] = ve.n
                    dump(Gq, dbg_h2, snap_h2[:, 0:BPW * TS], wait=(ve.h, rec["snap_h2"]))

                for pi in range(TPS * NDG):
                    tb, dwdw = pi // NDG, pi % NDG

                    def _(t, sl=sl, w=w, tb=tb, dwdw=dwdw, pi=pi):
                        if pi == 0:
                            t.wait_ge(ve.h, rec["h2_all"])
                            t.wait_ge(s_w2fw.h, rec[("w2fw_ld", w)])
                        t.wait_ge(ve.h, rec.get(("dn_used", pi % 2), 0))
                        for bl in range(BPW):
                            ins = t.matmul(
                                ps[4 + pi % 2][:, 0:KW],
                                h2b[w % 2][:, bl * TS + tb * 128:bl * TS + (tb + 1) * 128],
                                attnbuf[:, bl * D + dwdw * KW:bl * D + (dwdw + 1) * KW],
                                start=(bl == 0), stop=(bl == BPW - 1))
                        pe.tick(ins)
                    Tn(_)
                    rec["dn_mm"] = pe.n

                    def _(v, sl=sl, w=w, tb=tb, dwdw=dwdw, pi=pi):
                        v.wait_ge(pe.h, rec["dn_mm"])
                        ins = v.tensor_tensor(
                            xsl[tb][:, dwdw * KW:(dwdw + 1) * KW],
                            ps[4 + pi % 2][:, 0:KW],
                            xsl[tb][:, dwdw * KW:(dwdw + 1) * KW], ALU.add)
                        ve.tick(ins)
                    V(_)
                    rec[("dn_used", pi % 2)] = ve.n
                    if w == NW - 1:
                        rec[("dnadd_tb", tb)] = ve.n
                rec[("h2b_used", w % 2)] = pe.n
                rec[("h2ps_used", 0)] = ve.n
                rec[("h2ps_used", 1)] = ve.n
                if serial_ffn:
                    def _(t, sl=sl, w=w):
                        t.wait_ge(ve.h, ve.n)
                        t.wait_ge(ae.h, ae.n)
                    Tn(_)
                if debug and sl == 0 and w == 0:
                    dump(Gq, dbg_dn0, xsl[0][:, :], wait=(ve.h, ve.n))
            rec[("w2fw_used", sl)] = pe.n
            rec["dn_all"] = ve.n

            for tb in range(TPS):
                def _(a, sl=sl, tb=tb):
                    a.wait_ge(ve.h, rec[("dnadd_tb", tb)])
                    ins = a.dma_start(
                        out=out[sl * TS + tb * 128:sl * TS + (tb + 1) * 128, :],
                        in_=xsl[tb][:, :])
                    s_st.tick(ins, 16)
                Aq(_)
            rec["xst_slice"] = s_st.n
            rec["p2e_v"] = ve.n

        def _(a):
            a.wait_ge(s_st.h, s_st.n)
        Aq(_)

    return nc


_NC_CACHE = {}


def _get_nc(key):
    if key not in _NC_CACHE:
        _NC_CACHE[key] = build_nc(*key)
    return _NC_CACHE[key]


def make_consts(D, F):
    C = F // 128
    G = 128 // C
    return dict(
        identb=np.eye(128).astype(bfloat16),
        h128=_hadamard(128).astype(bfloat16),
        ighc=np.kron(np.eye(G), _hadamard(C)).astype(bfloat16),
        onesb=np.ones((128, 512), dtype=bfloat16),
        cst=np.tile(np.array([[1e-5, -0.5, 1.0 / math.sqrt(F), 0.0]],
                             dtype=np.float32), (128, 1)),
    )


def kernel(L0_attn_x, L0_attn_wq, L0_attn_bq, L0_attn_wk, L0_attn_bk,
           L0_attn_wv, L0_attn_bv, L0_attn_wo, L0_attn_bo,
           L0_ffn_W1, L0_ffn_W2, L0_ffn_W3):
    n_cores = 8
    x = np.ascontiguousarray(np.asarray(L0_attn_x, dtype=np.float32))
    B, Sq, D = x.shape
    T = (B * Sq) // n_cores
    F = np.asarray(L0_ffn_W1).shape[0]
    NH = D // 128
    TS = 512
    xf = x.reshape(B * Sq, D)

    w = {k: np.ascontiguousarray(np.asarray(v, dtype=np.float32)) for k, v in dict(
        wq=L0_attn_wq, wk=L0_attn_wk, wv=L0_attn_wv, wo=L0_attn_wo,
        w1=L0_ffn_W1, w2=L0_ffn_W2, w3=L0_ffn_W3).items()}
    b = {k: np.ascontiguousarray(np.asarray(v, dtype=np.float32).reshape(1, D))
         for k, v in dict(bq=L0_attn_bq, bk=L0_attn_bk, bv=L0_attn_bv,
                          bo=L0_attn_bo).items()}
    consts = make_consts(D, F)

    nc = _get_nc((T, D, NH, F, TS, n_cores))
    in_maps = []
    for c in range(n_cores):
        m = {"xin": np.ascontiguousarray(xf[c * T:(c + 1) * T])}
        m.update(w)
        m.update(b)
        m.update(consts)
        in_maps.append(m)
    res = run_bass_kernel_spmd(nc, in_maps, core_ids=list(range(n_cores)))
    outf = np.concatenate([np.asarray(r["out"]) for r in res.results], axis=0)
    return outf.reshape(B, Sq, D).astype(np.float32)
